# version 39
# baseline (speedup 1.0000x reference)
import math
import numpy as np
import ml_dtypes
from contextlib import ExitStack
import concourse.bass as bass
import concourse.mybir as mybir
import bass_rust as _bass_rust
from concourse.bass_utils import run_bass_kernel_spmd
from concourse.library_config import all_libraries, standard as _std_lib

F32 = mybir.dt.float32
BF16 = mybir.dt.bfloat16
U32 = mybir.dt.uint32
I16 = mybir.dt.int16
AX = mybir.AxisListType.X
AF = mybir.ActivationFunctionType
OP = mybir.AluOpType

FULL_CFG = dict(D=1024, E=8, H=4096, TOK=2048, SF=4096, PAD=640, CAP=5120,
                NCORES=8)


def build_program(cfg):
    D = cfg["D"]; E = cfg["E"]; H = cfg["H"]; TOK = cfg["TOK"]
    SF = cfg["SF"]; PAD = cfg["PAD"]; CAP = cfg["CAP"]; NCORES = cfg["NCORES"]
    assert E == 8
    DC = D // 128; HC = H // 128; NT = TOK // 128
    NBF = SF // 512; NOB = TOK // 512
    EPAD = E * PAD; NSL = PAD // 128; GIW = PAD // 16
    NQB = TOK // 512
    KC = SF // 128
    SCALE = 1.0 / math.sqrt(D)
    DW = min(D, 512); DHN = D // DW
    QH = 256
    SBW = PAD // 2 if PAD // 2 <= 512 else 512
    NSB = PAD // SBW
    HS = min(H, 1024); NHS = H // HS
    HSC = HS // 128

    nc = bass.Bass("TRN2", target_bir_lowering=False, debug=False)

    xo_h = nc.dram_tensor("x_own", [TOK, D], F32, kind="ExternalInput")
    xt_h = nc.dram_tensor("x_oth", [SF - TOK, D], F32, kind="ExternalInput")
    wqt_h = nc.dram_tensor("wq_t", [D, D], F32, kind="ExternalInput")
    wkt_h = nc.dram_tensor("wk_t", [D, D], F32, kind="ExternalInput")
    wvt_h = nc.dram_tensor("wv_t", [D, D], F32, kind="ExternalInput")
    wot_h = nc.dram_tensor("wo_t", [D, D], F32, kind="ExternalInput")
    wg_h = nc.dram_tensor("wg", [D, E], F32, kind="ExternalInput")
    fc1s_h = nc.dram_tensor("fc1s", [E * (H // min(H, 1024)) * 128,
                                     (D // 128) * min(H, 1024)], F32,
                            kind="ExternalInput")
    fc2r_h = nc.dram_tensor("fc2r", [E * 128, (H // 128) * D], F32,
                            kind="ExternalInput")
    fc1b_h = nc.dram_tensor("fc1b", [E, H], F32, kind="ExternalInput")
    fc2b_h = nc.dram_tensor("fc2b", [E, D], F32, kind="ExternalInput")
    lng_h = nc.dram_tensor("ln_g", [1, D], F32, kind="ExternalInput")
    lnb_h = nc.dram_tensor("ln_b", [1, D], F32, kind="ExternalInput")
    tri_h = nc.dram_tensor("tri", [128, 128], F32, kind="ExternalInput")
    ident_h = nc.dram_tensor("ident", [128, 128], F32, kind="ExternalInput")
    e8_h = nc.dram_tensor("e8", [128, E], F32, kind="ExternalInput")
    epad_h = nc.dram_tensor("epad_bc", [128, 2 * E], F32, kind="ExternalInput")
    pmask_h = nc.dram_tensor("pmask", [1, 16 * NCORES], F32, kind="ExternalInput")
    gidx_h = nc.dram_tensor("gidx", [128, E * GIW], I16, kind="ExternalInput")
    out_h = nc.dram_tensor("out", [TOK, D], F32, kind="ExternalOutput")

    disp_d = nc.dram_tensor("disp_d", [EPAD, D], BF16)
    ybuf_d = nc.dram_tensor("ybuf_d", [EPAD, D], BF16)
    xnsp_d = nc.dram_tensor("xnsp_d", [TOK, D], BF16)
    qsp_d = nc.dram_tensor("qsp_d", [DC, 128, TOK], BF16)
    aosp_d = nc.dram_tensor("aosp_d", [TOK, D], BF16)
    cci_d = nc.dram_tensor("cci_d", [1, 2 * E], F32)
    cco_d = nc.dram_tensor("cco_d", [NCORES, 2 * E], F32)

    stk = ExitStack()
    stkA = ExitStack()
    stkB = ExitStack()
    stkC = ExitStack()
    SB = lambda n, s, d: stk.enter_context(nc.sbuf_tensor(n, s, d))
    SBA = lambda n, s, d: stkA.enter_context(nc.sbuf_tensor(n, s, d))
    SBB = lambda n, s, d: stkB.enter_context(nc.sbuf_tensor(n, s, d))
    SBC = lambda n, s, d: stkC.enter_context(nc.sbuf_tensor(n, s, d))
    PS = lambda n, s, d: stk.enter_context(nc.psum_tensor(n, s, d))

    class Sem:

        def __init__(self, name):
            self.h = stk.enter_context(nc.semaphore(name)); self.n = 0

        def inc(self, instr, k=1):
            instr.then_inc(self.h, k); self.n += k; return self.n

        def dma(self, instr):
            return self.inc(instr, 16)

    vc = Sem("vc"); acx = Sem("acx"); tcx = Sem("tcx"); gcc = Sem("gcc")
    ccs = Sem("ccs")
    s_cst = Sem("s_cst"); s_pre = Sem("s_pre"); s_ccl = Sem("s_ccl")
    s_wol = Sem("s_wol")
    s_bld = Sem("s_bld")
    s_xld = [Sem(f"s_xld{i}") for i in range(2)]
    s_xnw = [Sem(f"s_xnw{i}") for i in range(2)]
    s_qst = [Sem(f"s_qst{i}") for i in range(2)]
    s_qsl = [Sem(f"s_qsl{i}") for i in range(2)]
    s_aos = [Sem(f"s_aos{i}") for i in range(2)]
    s_yst = [Sem(f"s_yst{i}") for i in range(2)]
    s_x5a = [Sem(f"s_x5a{i}") for i in range(2)]
    s_x5c = [Sem(f"s_x5c{i}") for i in range(2)]
    s_ost = [Sem(f"s_ost{i}") for i in range(2)]
    s_xrl = [Sem(f"s_xrl{i}") for i in range(2)]
    s_xsc = [Sem(f"s_xsc{i}") for i in range(2)]
    g_cst = Sem("g_cst"); g_cci = Sem("g_cci")
    g_w3p = [Sem(f"g_w3p{i}") for i in range(4)]
    g_f2q = [Sem(f"g_f2q{i}") for i in range(4)]
    g_sct = [Sem(f"g_sct{i}") for i in range(2)]
    g_xsl = Sem("g_xsl")
    g_xld = [Sem(f"g_xld{i}") for i in range(2)]
    g_gth = [Sem(f"g_gth{i}") for i in range(2)]
    g_f1l = [Sem(f"g_f1l{i}") for i in range(3)]
    g_ygl = [Sem(f"g_ygl{i}") for i in range(2)]

    ident = SB("ident_sb", [128, 128], BF16)
    onesc = SB("onesc", [128, 1], BF16)
    gates = SB("gates_sb", [128, NT * 2], F32)
    slots = SB("slots_sb", [128, NT * 2], U32)
    sums = SB("sums_sb", [128, NT], F32)
    mvst = SB("mvst", [128, 16], F32)
    xns = [SBB("xns0", [128, D], BF16)]
    kf = SBB("kf", [128, DC * SF], BF16)
    vt = SBB("vt", [128, KC * D], BF16)
    onesr = SBA("onesr", [1, 128], F32)
    tri = SBA("tri_sb", [128, 128], BF16)
    e8 = SBA("e8_sb", [128, E], F32)
    epad = SBA("epad_sb", [128, 2 * E], F32)
    pmask = SBA("pmask_sb", [1, 16 * NCORES], F32)
    g0buf = SBA("g0buf", [128, NT * 2], F32)
    masksf = SBA("masksf", [128, NT * 16], BF16)
    totrow = SBA("totrow", [1, 16 * (NT + 1)], F32)
    offrow = SBA("offrow", [1, 16 * NT], F32)
    rows48 = SBA("rows48", [1, 48 * NT], F32)
    cntloc = SBA("cntloc", [1, 16], F32)
    ccrow = SBA("ccrow", [1, 16 * NCORES], F32)
    masked = SBA("maskedcc", [1, 16 * NCORES], F32)
    rowa = SBA("rowa", [1, 16], F32)
    logt = SBA("logt", [128, NT * E], BF16)
    top8 = SBA("top8_sb", [128, 8], F32)
    topi = SBA("topi_sb", [128, 8], U32)
    rkA = SBA("rkA", [128, (NT // 4) * 48], F32)
    tA = SBA("tA", [128, (NT // 4) * 16], F32)
    k2a = SBA("k2a", [128, NT * 2], F32)
    idxall = SBA("idxall", [128, 2 * NT], F32)
    g0a = SBA("g0a", [128, NT], F32)
    g0s = SBA("g0s", [128, NT], F32)

    ps0 = PS("ps0", [128, 512], F32)
    ps1 = PS("ps1", [128, 512], F32)
    ps2 = PS("ps2", [128, 512], F32)
    ps3 = PS("ps3", [128, 512], F32)
    pst0 = PS("pst0", [128, 1024], BF16)
    pst1 = PS("pst1", [128, 1024], BF16)
    psr = PS("psr", [128, 512], F32)
    psw = PS("psw", [128, 512], F32)
    pspool = [ps0, ps1, ps2, ps3]

    st = {}

    def W(eng, ev, default=None):
        v = st.get(ev, default)
        if v is None:
            return
        eng.wait_ge(v[0].h, v[1])

    zsrc = kf[:, DC * SF - 512: DC * SF]
    nc.gpsimd.memset(zsrc, 0)
    gcc.inc(nc.gpsimd.engine_nop(), 1)
    g_cst.dma(nc.gpsimd.dma_start(out=tri[:, :], in_=tri_h[:, :]))
    g_cst.dma(nc.gpsimd.dma_start(out=ident[:, :], in_=ident_h[:, :]))
    st[("ident",)] = (g_cst, g_cst.n)

    s_cst.dma(nc.sync.dma_start(out=e8[:, :], in_=e8_h[:, :]))
    s_cst.dma(nc.sync.dma_start(out=epad[:, :], in_=epad_h[:, :]))
    s_cst.dma(nc.sync.dma_start(out=pmask[:, :], in_=pmask_h[:, :]))
    SQ_CONST = s_cst.n

    nc.vector.memset(onesr[:, :], 1.0)
    nc.vector.memset(onesc[:, :], 1.0)
    nc.vector.memset(totrow[:, :], 0.0)
    nc.vector.memset(offrow[:, :], 0.0)
    vc.inc(nc.vector.memset(rows48[:, :], 0.0), 1)
    VC_CONST = vc.n

    nc.sync.wait_ge(gcc.h, 1)
    pre_jobs = [(t, hv) for t in range(EPAD // 128)
                for hv in range((D + 511) // 512)]

    def emit_prefill_chunk(k, nchunks):
        per = -(-len(pre_jobs) // nchunks)
        for t, hv in pre_jobs[k * per:(k + 1) * per]:
            w_ = min(512, D - hv * 512)
            s_pre.dma(nc.sync.dma_start(
                out=disp_d[t * 128:(t + 1) * 128, hv * 512:hv * 512 + w_],
                in_=zsrc[:, 0:w_]))

    w3t = SBA("w3t", [128, 3 * DC * D], BF16)
    wgsb = SBA("wgsb", [128, DC * E], BF16)
    xtile = [SBA(f"xtile{i}", [128, D], BF16) for i in range(2)]
    xnf = [SBA("xnf0", [128, DC * 512], BF16)]
    xnt = [SBA(f"xnt{i}", [128, D], BF16) for i in range(2)]
    qevb = [SBA("qevb0", [128, 512], BF16), None]
    qevb1 = SBA("qevb1", [128, 512], BF16)
    logp = SBA("logp", [128, 512], BF16)

    w3_pieces = [(1, 0), (1, 1), (0, 0), (0, 1), (3, 0), (2, 0), (2, 1)]
    w3_keys = {1: ("w3k",), 0: ("w3q",), 2: ("w3v",)}

    def emit_w3_piece(k):
        if k >= len(w3_pieces):
            return
        wi, hh = w3_pieces[k]
        sem = g_w3p[wi]
        if wi == 3:
            sem.dma(nc.gpsimd.dma_start(
                out=wgsb[:, :].rearrange("p (c e) -> p c e", c=DC),
                in_=bass.AP(wg_h, 0, [[E, 128], [128 * E, DC], [1, E]])))
            st[("wg",)] = (sem, sem.n)
            return
        wh = [wqt_h, wkt_h, wvt_h][wi]
        DH2 = DC // 2
        sem.dma(nc.gpsimd.dma_start(
            out=w3t[:, wi * DC * D + hh * DH2 * D:
                    wi * DC * D + (hh + 1) * DH2 * D].rearrange(
                "p (c o) -> p c o", c=DH2),
            in_=bass.AP(wh, hh * DH2 * 128 * D, [[D, 128], [128 * D, DH2], [1, D]])))
        if hh == 1:
            st[w3_keys[wi]] = (sem, sem.n)

    blocks = [(xo_h, tb * 512, True) if tb < NOB else
              (xt_h, (tb - NOB) * 512, False) for tb in range(NBF)]

    def emit_p2a1(lo=0, hi=None):
        hi = NT if hi is None else hi
        if lo == 0:
            nc.vector.wait_ge(s_cst.h, SQ_CONST)
        for t in range(lo, hi):
            W(nc.vector, ("logt", t))
            vc.inc(nc.vector.max(top8[:, :], logt[:, t * E:(t + 1) * E]), 1)
            nc.vector.wait_ge(vc.h, vc.n)
            vc.inc(nc.vector.max_index(topi[:, :], top8[:, :],
                                       logt[:, t * E:(t + 1) * E]), 1)
            nc.vector.wait_ge(vc.h, vc.n)
            vc.inc(nc.vector.tensor_copy(idxall[:, 2 * t:2 * t + 2],
                                         topi[:, 0:2]), 1)
            vc.inc(nc.vector.tensor_sub(g0a[:, t:t + 1], top8[:, 0:1],
                                        top8[:, 1:2]), 1)
            nc.vector.wait_ge(vc.h, vc.n)
        st[('p2a1_hi',)] = hi

    def emit_p2a1b():
        nc.scalar.wait_ge(vc.h, vc.n)
        acx.inc(nc.scalar.activation(g0s[:, :], g0a[:, :], AF.Sigmoid), 1)
        nc.vector.wait_ge(acx.h, acx.n)
        vc.inc(nc.vector.tensor_copy(
            bass.AP(g0buf, 0, [[2 * NT, 128], [2, NT]]), g0s[:, :]), 1)
        vc.inc(nc.vector.tensor_scalar(
            bass.AP(g0buf, 1, [[2 * NT, 128], [2, NT]]), g0s[:, :], -1.0, 1.0,
            op0=OP.mult, op1=OP.add), 1)
        nc.vector.wait_ge(vc.h, vc.n)
        for t in range(NT):
            for j in range(2):
                vc.inc(nc.vector.tensor_scalar(
                    masksf[:, t * 16 + j * 8: t * 16 + (j + 1) * 8], e8[:, :],
                    idxall[:, 2 * t + j:2 * t + j + 1], None, op0=OP.is_equal), 1)
            st[("mask", t)] = (vc, vc.n)
        st[('p2a1b_done',)] = True

    def emit_p2a2():
        for t in range(NT):
            W(nc.tensor, ("mask", t))
            W(nc.tensor, ("pswfree2",), st.get(("pswfree",)))
            mm = nc.tensor.matmul(psw[0:1, 0:16], onesc[:, 0:1],
                                  masksf[:, t * 16:(t + 1) * 16],
                                  start=True, stop=True)
            tcx.inc(mm, 1)
            nc.vector.wait_ge(tcx.h, tcx.n)
            vc.inc(nc.vector.tensor_copy(totrow[:, (t + 1) * 16:(t + 2) * 16],
                                         psw[0:1, 0:16]), 1)
            st[("pswfree2",)] = (vc, vc.n)
        for t in range(NT):
            if t == 0:
                vc.inc(nc.vector.tensor_copy(offrow[:, 0:16], totrow[:, 0:16]), 1)
            else:
                vc.inc(nc.vector.tensor_add(
                    offrow[:, t * 16:(t + 1) * 16],
                    offrow[:, (t - 1) * 16: t * 16],
                    totrow[:, t * 16:(t + 1) * 16]), 1)
            nc.vector.wait_ge(vc.h, vc.n)
        vc.inc(nc.vector.tensor_add(cntloc[:, :],
                                    offrow[:, (NT - 1) * 16: NT * 16],
                                    totrow[:, NT * 16:(NT + 1) * 16]), 1)
        st[("cnt",)] = (vc, vc.n)

        st[('p2a2_done',)] = True

    def emit_p2a2c():
        W(nc.gpsimd, ("cnt",))
        g_cci.dma(nc.gpsimd.dma_start(out=cci_d[:, :], in_=cntloc[:, :]))
        nc.gpsimd.wait_ge(g_cci.h, g_cci.n)
        ccs.inc(nc.gpsimd.collective_compute(
            "AllGather", OP.bypass, replica_groups=[list(range(NCORES))],
            ins=[cci_d.ap().opt()], outs=[cco_d.ap().opt()]), 1)
        st[('p2a2c_done',)] = True

    def emit_p2bh():
        if st.get(('p2a1_hi',), 0) < NT:
            emit_p2a1(st.get(('p2a1_hi',), 0), NT)
        if ('p2a1b_done',) not in st:
            emit_p2a1b()
        if ('p2a2_done',) not in st:
            emit_p2a2()
        if ('p2a2c_done',) not in st:
            emit_p2a2c()
        nc.sync.wait_ge(ccs.h, ccs.n)
        s_ccl.dma(nc.sync.dma_start(
            out=ccrow[0:1, :],
            in_=bass.AP(cco_d, 0, [[16 * NCORES, 1], [1, 16 * NCORES]])))

        nc.vector.wait_ge(s_ccl.h, s_ccl.n)
        vc.inc(nc.vector.tensor_mul(masked[:, :], ccrow[:, :], pmask[:, :]), 1)
        nc.vector.wait_ge(vc.h, vc.n)
        vc.inc(nc.vector.tensor_reduce(
            rowa[:, 0:16],
            bass.AP(masked, 0, [[16 * NCORES, 1], [1, 16], [16, NCORES]]),
            axis=AX, op=OP.add), 1)
        nc.vector.wait_ge(vc.h, vc.n)
        vc.inc(nc.vector.tensor_reduce(
            mvst[0:1, 0:E],
            bass.AP(ccrow, 0, [[16 * NCORES, 1], [1, E], [16, NCORES]]),
            axis=AX, op=OP.add), 1)
        nc.vector.wait_ge(vc.h, vc.n)
        vc.inc(nc.vector.tensor_add(rowa[:, E:16], rowa[:, E:16],
                                    mvst[0:1, 0:E]), 1)
        nc.vector.wait_ge(vc.h, vc.n)
        for t in range(NT):
            vc.inc(nc.vector.tensor_copy(rows48[:, t * 48: t * 48 + 16],
                                         offrow[:, t * 16:(t + 1) * 16]), 1)
            vc.inc(nc.vector.tensor_copy(rows48[:, t * 48 + 16: t * 48 + 32],
                                         rowa[:, 0:16]), 1)
            vc.inc(nc.vector.tensor_copy(rows48[:, t * 48 + 40: t * 48 + 48],
                                         cntloc[:, 0:E]), 1)
            nc.vector.wait_ge(vc.h, vc.n)
            vc.inc(nc.vector.tensor_add(rows48[:, t * 48 + 32: t * 48 + 48],
                                        rows48[:, t * 48 + 32: t * 48 + 48],
                                        epad[0:1, :]), 1)
        nc.vector.wait_ge(vc.h, vc.n)
        st[('rows_ok',)] = vc.n
        st[('p2bh_done',)] = True

    def emit_p2b():
        if ('p2bh_done',) not in st:
            emit_p2bh()
        ROWS_OK = st[('rows_ok',)]

        NTH = NT // 4
        for hb in range(4):
            for tt in range(NTH):
                t = hb * NTH + tt
                nc.tensor.wait_ge(vc.h, ROWS_OK)
                W(nc.tensor, ("psrfree",))
                if hb:
                    W(nc.tensor, ("rkAfree",))
                nc.tensor.matmul(psr[:, 0:48], onesr[0:1, 0:128],
                                 rows48[:, t * 48:(t + 1) * 48], start=True, stop=False)
                mm = nc.tensor.matmul(psr[:, 0:16], tri[:, :],
                                      masksf[:, t * 16:(t + 1) * 16],
                                      start=False, stop=True)
                tcx.inc(mm, 1)
                nc.vector.wait_ge(tcx.h, tcx.n)
                vc.inc(nc.vector.tensor_copy(rkA[:, tt * 48:(tt + 1) * 48],
                                             psr[:, 0:48]), 1)
                st[("psrfree",)] = (vc, vc.n)
            nc.vector.wait_ge(vc.h, vc.n)
            v3 = lambda off: bass.AP(rkA, off, [[NTH * 48, 128], [48, NTH], [1, 16]])
            tAv = bass.AP(tA, 0, [[NTH * 16, 128], [16, NTH], [1, 16]])
            mfh = masksf[:, hb * NTH * 16:(hb + 1) * NTH * 16]
            vc.inc(nc.vector.tensor_tensor(tAv, v3(0), v3(16), op=OP.add), 1)
            nc.vector.wait_ge(vc.h, vc.n)
            vc.inc(nc.vector.tensor_single_scalar(tA[:, :], tA[:, :],
                                                  float(CAP), OP.is_lt), 1)
            nc.vector.wait_ge(vc.h, vc.n)
            vc.inc(nc.vector.tensor_mul(tA[:, :], tA[:, :], mfh), 1)
            nc.vector.wait_ge(vc.h, vc.n)
            vc.inc(nc.vector.tensor_reduce(
                gates[:, hb * NTH * 2:(hb + 1) * NTH * 2],
                bass.AP(tA, 0, [[NTH * 16, 128], [8, NTH * 2], [1, 8]]),
                axis=AX, op=OP.add), 1)
            nc.vector.wait_ge(vc.h, vc.n)
            vc.inc(nc.vector.tensor_mul(gates[:, hb * NTH * 2:(hb + 1) * NTH * 2],
                                        gates[:, hb * NTH * 2:(hb + 1) * NTH * 2],
                                        g0buf[:, hb * NTH * 2:(hb + 1) * NTH * 2]), 1)
            vc.inc(nc.vector.tensor_tensor(tAv, v3(0), v3(32), op=OP.add), 1)
            nc.vector.wait_ge(vc.h, vc.n)
            vc.inc(nc.vector.tensor_mul(tA[:, :], tA[:, :], mfh), 1)
            nc.vector.wait_ge(vc.h, vc.n)
            vc.inc(nc.vector.tensor_reduce(
                k2a[:, hb * NTH * 2:(hb + 1) * NTH * 2],
                bass.AP(tA, 0, [[NTH * 16, 128], [8, NTH * 2], [1, 8]]),
                axis=AX, op=OP.add), 1)
            nc.vector.wait_ge(vc.h, vc.n)
            st[("rkAfree",)] = (vc, vc.n)
            vc.inc(nc.vector.tensor_scalar_min(
                k2a[:, hb * NTH * 2:(hb + 1) * NTH * 2],
                k2a[:, hb * NTH * 2:(hb + 1) * NTH * 2], float(EPAD - 1)), 1)
            nc.vector.wait_ge(vc.h, vc.n)
            vc.inc(nc.vector.tensor_copy(slots[:, hb * NTH * 2:(hb + 1) * NTH * 2],
                                         k2a[:, hb * NTH * 2:(hb + 1) * NTH * 2]), 1)
            nc.vector.wait_ge(vc.h, vc.n)
        SLOTS_OK = vc.n
        for t in range(NT):
            st[("slot", t)] = (vc, SLOTS_OK)
        st[('P2_VC',)] = vc.n

        st[('p2_done',)] = True

    def emit_scatters():
        for t in range(NT):
            xs = 0
            W(nc.gpsimd, ("xns_scat", xs))
            W(nc.gpsimd, ("xnsp", t))
            g_xsl.dma(nc.gpsimd.dma_start(out=xns[xs][:, :],
                                          in_=xnsp_d[t * 128:(t + 1) * 128, :]))
            st[("xnrl", t)] = (g_xsl, g_xsl.n)
            W(nc.gpsimd, ("xnrl", t))
            W(nc.gpsimd, ("slot", t))
            if t == 0:
                nc.gpsimd.wait_ge(s_pre.h, s_pre.n)
            for j in range(2):
                g_sct[xs].dma(nc.gpsimd.indirect_dma_start(
                    out=disp_d[:, :],
                    out_offset=bass.IndirectOffsetOnAxis(
                        ap=slots[:, t * 2 + j: t * 2 + j + 1], axis=0),
                    in_=xns[xs][:, :], in_offset=None))
            st[("xns_scat", xs)] = (g_sct[xs], g_sct[xs].n)


    NGTS = 4 * len(blocks)
    fslot = 0

    def emit_ln(gt):
        src_h, roff, own = blocks[gt // 4]
        i = gt % 4
        xs = gt % 2
        W(nc.gpsimd, ("xtile_free", xs))
        g_xld[xs].dma(nc.gpsimd.dma_start(
            out=xtile[xs][:, :],
            in_=src_h[roff + i * 128: roff + (i + 1) * 128, :]))
        st[("xload", gt)] = (g_xld[xs], g_xld[xs].n)

        W(nc.vector, ("xload", gt))
        nst = (D + 511) // 512
        for c_ in range(nst):
            w_ = min(512, D - c_ * 512)
            vc.inc(nc.vector.bn_stats(
                mvst[:, 6 * c_:6 * c_ + 6],
                xtile[xs][:, c_ * 512: c_ * 512 + w_]), 1)
            nc.vector.wait_ge(vc.h, vc.n)
        vc.inc(nc.vector.bn_aggr(mvst[:, 12:14], mvst[:, 0:6 * nst]), 1)
        nc.vector.wait_ge(vc.h, vc.n)
        vc.inc(nc.vector.tensor_scalar_add(mvst[:, 14:15], mvst[:, 13:14], 1e-5), 1)
        nc.scalar.wait_ge(vc.h, vc.n)
        acx.inc(nc.scalar.activation(mvst[:, 15:16], mvst[:, 14:15], AF.Sqrt), 1)
        nc.vector.wait_ge(acx.h, acx.n)
        vc.inc(nc.vector.reciprocal(mvst[:, 15:16], mvst[:, 15:16]), 1)
        nc.vector.wait_ge(vc.h, vc.n)
        xslot = gt % 2
        W(nc.vector, ("xnt_free", xslot))
        W(nc.vector, ("xnt_spill", xslot))
        vc.inc(nc.vector.tensor_scalar(
            xnt[xslot][:, :], xtile[xs][:, :], mvst[:, 12:13], mvst[:, 15:16],
            op0=OP.subtract, op1=OP.mult), 1)
        st[("xn", gt)] = (vc, vc.n)
        st[("xtile_free", xs)] = (vc, vc.n)

        if own:
            W(nc.sync, ("xn", gt))
            s_xnw[xslot].dma(nc.sync.dma_start(
                out=xnsp_d[gt * 128:(gt + 1) * 128, :], in_=xnt[xslot][:, :]))
            st[("xnt_spill", xslot)] = (s_xnw[xslot], s_xnw[xslot].n)
            st[("xnsp", gt)] = (s_xnw[xslot], s_xnw[xslot].n)

    for tb, (src_h, roff, own) in enumerate(blocks):
        if tb >= 1:
            emit_prefill_chunk(tb - 1, max(1, len(blocks) - 1))
        if cfg.get("p2a_early", True) and tb == NOB + 2:
            if st.get(('p2a1_hi',), 0) < NT:
                emit_p2a1(st.get(('p2a1_hi',), 0), NT)
            emit_p2a1b()
            emit_p2a2()
        for i in range(4):
            gt = tb * 4 + i
            xslot = gt % 2
            if gt == 0:
                emit_ln(0)
                emit_ln(1)
                W(nc.tensor, ("ident",))

            W(nc.tensor, ("xn", gt))
            pstile = [pst0, pst1][gt % 2]
            W(nc.tensor, ("trev", gt % 2))
            for dcp in range(DC):
                tcx.inc(nc.tensor.transpose(
                    pstile[:, dcp * 128:(dcp + 1) * 128],
                    xnt[xslot][:, dcp * 128:(dcp + 1) * 128],
                    ident[:, :]), 1)
            st[("tr", gt)] = (tcx, tcx.n)
            W(nc.vector, ("tr", gt))
            vc.inc(nc.vector.tensor_copy(
                bass.AP(xnf[fslot], i * 128, [[DC * 512, 128], [512, DC], [1, 128]]),
                pstile[:, 0:DC * 128].rearrange("p (c k) -> p c k", c=DC)), 1)
            st[("trev", gt % 2)] = (vc, vc.n)
            st[("xnt_free", xslot)] = (tcx, tcx.n)
            st[("xnf", tb, i)] = (vc, vc.n)
            if gt + 2 < NGTS:
                emit_ln(gt + 2)
            if gt < 4:
                emit_w3_piece(2 * gt)
                emit_w3_piece(2 * gt + 1)

            if cfg.get("p2a_early", True):
                q4 = NT // 4 if NT >= 4 else 0
                if q4 and tb == NOB + 1:
                    emit_p2a1(i * q4, (i + 1) * q4)
                if tb == NOB + 2 and i == 0 and ('p2a2_done',) in st:
                    emit_p2a2c()

        for i in range(4):
            W(nc.tensor, ("xnf", tb, i))
        pi = 0
        wlist = [(1, "k")] + ([(0, "q")] if own else [])
        for wsel, nm in wlist:
            W(nc.tensor, ("w3k",) if nm == "k" else ("w3q",))
            for oc in range(DC):
                p = pspool[pi % 4]; pi += 1
                W(nc.tensor, ("psfree", p.name))
                for hc in range(DC):
                    mm = nc.tensor.matmul(
                        p[:, 0:512],
                        w3t[:, wsel * DC * D + hc * D + oc * 128:
                            wsel * DC * D + hc * D + oc * 128 + 128],
                        xnf[fslot][:, hc * 512:(hc + 1) * 512],
                        start=(hc == 0), stop=(hc == DC - 1))
                tcx.inc(mm, 1)
                nc.vector.wait_ge(tcx.h, tcx.n)
                if nm == "k":
                    if oc == DC - 1 and tb == NBF - 1:
                        nc.vector.wait_ge(s_pre.h, s_pre.n)
                    vc.inc(nc.vector.tensor_copy(
                        kf[:, oc * SF + tb * 512: oc * SF + (tb + 1) * 512],
                        p[:, 0:512]), 1)
                else:
                    qb_ = oc % 2
                    qt_ = qevb[0] if qb_ == 0 else qevb1
                    W(nc.vector, ("qevb_used", qb_))
                    vc.inc(nc.vector.tensor_copy(qt_[:, :], p[:, 0:512]), 1)
                    nc.sync.wait_ge(vc.h, vc.n)
                    s_qst[qb_].dma(nc.sync.dma_start(
                        out=qsp_d[oc, :, tb * 512:(tb + 1) * 512],
                        in_=qt_[:, :]))
                    st[("qevb_used", qb_)] = (s_qst[qb_], s_qst[qb_].n)
                st[("psfree", p.name)] = (vc, vc.n)
        W(nc.tensor, ("w3v",))
        for tc4 in range(4):
            for dh in range(DHN):
                p = pspool[pi % 4]; pi += 1
                W(nc.tensor, ("psfree", p.name))
                for hc in range(DC):
                    mm = nc.tensor.matmul(
                        p[:, 0:DW],
                        xnf[fslot][:, hc * 512 + tc4 * 128: hc * 512 + (tc4 + 1) * 128],
                        w3t[:, 2 * DC * D + hc * D + dh * DW:
                            2 * DC * D + hc * D + (dh + 1) * DW],
                        start=(hc == 0), stop=(hc == DC - 1))
                tcx.inc(mm, 1)
                kc = tb * 4 + tc4
                nc.vector.wait_ge(tcx.h, tcx.n)
                vc.inc(nc.vector.tensor_copy(
                    vt[:, kc * D + dh * DW: kc * D + (dh + 1) * DW],
                    p[:, 0:DW]), 1)
                st[("psfree", p.name)] = (vc, vc.n)
        if own:
            W(nc.tensor, ("wg",))
            W(nc.tensor, ("pswfree",))
            for hc in range(DC):
                mm = nc.tensor.matmul(
                    psw[0:E, 0:512], wgsb[:, hc * E:(hc + 1) * E],
                    xnf[fslot][:, hc * 512:(hc + 1) * 512],
                    start=(hc == 0), stop=(hc == DC - 1))
            tcx.inc(mm, 1)
            nc.vector.wait_ge(tcx.h, tcx.n)
            vc.inc(nc.vector.tensor_copy(logp[0:E, :], psw[0:E, 0:512]), 1)
            st[("pswfree",)] = (vc, vc.n)
            st[("logev", tb)] = (vc, vc.n)
            W(nc.tensor, ("logev", tb))
            for i in range(4):
                gt = tb * 4 + i
                pstile = [pst0, pst1][i % 2]
                W(nc.tensor, ("trev", i % 2))
                tcx.inc(nc.tensor.transpose(
                    pstile[0:128, 0:E], logp[0:E, i * 128:(i + 1) * 128],
                    ident[0:E, 0:E]), 1)
                st[("ltr", gt)] = (tcx, tcx.n)
                W(nc.vector, ("ltr", gt))
                vc.inc(nc.vector.tensor_copy(
                    logt[:, gt * E:(gt + 1) * E], pstile[0:128, 0:E]), 1)
                st[("trev", i % 2)] = (vc, vc.n)
                st[("logt", gt)] = (vc, vc.n)
        st[("xnf_free", fslot)] = (tcx, tcx.n)

    emit_p2b()
    P1_TC = tcx.n
    P1_VC = vc.n
    stkA.close()

    wot = SBB("wot", [128, DC * D], BF16)
    qtile = [SBB(f"qtile{i}", [128, DC * 128], BF16) for i in range(2)]
    scoresb = SBB("scoresb", [128, SF], F32)
    probs = [SBB(f"probsb{i}", [128, SF], BF16) for i in range(2)]
    probsT = SBB("probsT", [128, KC * QH], BF16)
    ctxf = SBB("ctxf", [128, DC * QH], BF16)
    aot = [SBB("aot0", [128, 512], BF16)]
    NKC = SF // 512
    cmax = SBB("cmax", [128, 2 * NKC], F32)

    nc.gpsimd.wait_ge(vc.h, st[('P2_VC',)])
    s_wol.dma(nc.gpsimd.dma_start(
        out=wot[:, :].rearrange("p (c o) -> p c o", c=DC),
        in_=bass.AP(wot_h, 0, [[D, 128], [128 * D, DC], [1, D]])))
    emit_scatters()
    SCAT = [(g_sct[0], g_sct[0].n), (g_sct[1], g_sct[1].n)]

    nc.tensor.wait_ge(vc.h, P1_VC)

    def emit_scores(gt, qs):
        gp_ = gt % 2
        W(nc.tensor, ("qldt", gt))
        for kc8 in range(SF // 512):
            p = pspool[kc8 % 2]
            W(nc.tensor, ("psfree", p.name))
            for hc in range(DC):
                mm = nc.tensor.matmul(
                    p[:, 0:512],
                    qtile[qs][:, hc * 128:(hc + 1) * 128],
                    kf[:, hc * SF + kc8 * 512: hc * SF + (kc8 + 1) * 512],
                    start=(hc == 0), stop=(hc == DC - 1))
            tcx.inc(mm, 1)
            nc.vector.wait_ge(tcx.h, tcx.n)
            if kc8 == 0:
                W(nc.vector, ("exp_prev",))
            vc.inc(nc.vector.tensor_copy(
                scoresb[:, kc8 * 512:(kc8 + 1) * 512], p[:, 0:512]), 1)
            vc.inc(nc.vector.tensor_reduce(
                cmax[:, gp_ * NKC + kc8: gp_ * NKC + kc8 + 1],
                p[:, 0:512], axis=AX, op=OP.max), 1)
            st[("psfree", p.name)] = (vc, vc.n)
        st[("scev", gt)] = (vc, vc.n)
        st[("qtile_used", gt % 2)] = (tcx, tcx.n)

    def emit_softmax(gt):
        gp_ = gt % 2
        W(nc.vector, ("scev", gt))
        W(nc.vector, ("exp_prev2", gp_))
        vc.inc(nc.vector.reduce_max(mvst[:, 4 * gp_:4 * gp_ + 1],
                                    cmax[:, gp_ * NKC: gp_ * NKC + NKC], axis=AX), 1)
        nc.vector.wait_ge(vc.h, vc.n)
        vc.inc(nc.vector.tensor_scalar_mul(
            mvst[:, 4 * gp_ + 1:4 * gp_ + 2],
            mvst[:, 4 * gp_:4 * gp_ + 1], -SCALE), 1)
        nc.scalar.wait_ge(vc.h, vc.n)
        W(nc.scalar, ("ptr_last", gp_))
        acx.inc(nc.scalar.activation(
            probs[gp_][:, :], scoresb[:, :], AF.Exp,
            bias=mvst[:, 4 * gp_ + 1:4 * gp_ + 2],
            scale=SCALE, accum_out=mvst[:, 4 * gp_ + 2:4 * gp_ + 3]), 1)
        st[("exp", gt)] = (acx, acx.n)
        st[("exp_prev",)] = (acx, acx.n)
        st[("exp_prev2", gp_)] = (acx, acx.n)
        W(nc.vector, ("exp", gt))
        vc.inc(nc.vector.reciprocal(sums[:, gt:gt + 1],
                                    mvst[:, 4 * gp_ + 2:4 * gp_ + 3]), 1)

    def emit_transposes(gt):
        gp_ = gt % 2
        ti = gt % 2
        W(nc.tensor, ("exp", gt))
        for kb in range(0, KC, 8):
            nb = min(8, KC - kb)
            pstile = [pst0, pst1][(kb // 8) % 2]
            W(nc.tensor, ("trev", (kb // 8) % 2))
            if kb == 0:
                W(nc.tensor, ("probsT_done",))
            for k2_ in range(nb):
                kc = kb + k2_
                tcx.inc(nc.tensor.transpose(
                    pstile[:, k2_ * 128:(k2_ + 1) * 128],
                    probs[gp_][:, kc * 128:(kc + 1) * 128],
                    ident[:, :]), 1)
            st[("ptr", gt, kb)] = (tcx, tcx.n)
            W(nc.vector, ("ptr", gt, kb))
            vc.inc(nc.vector.tensor_copy(
                bass.AP(probsT, kb * QH + ti * 128,
                        [[KC * QH, 128], [QH, nb], [1, 128]]),
                pstile[:, 0:nb * 128].rearrange("p (k c) -> p k c", k=nb)), 1)
            st[("trev", (kb // 8) % 2)] = (vc, vc.n)
        st[("ptr_done", gt)] = (vc, vc.n)
        st[("ptr_last", gp_)] = st[("ptr", gt, KC - 8 if KC >= 8 else 0)]

    def emit_ctx_ao(h):
        W(nc.tensor, ("ptr_done", 2 * h))
        W(nc.tensor, ("ptr_done", 2 * h + 1))
        for hc in range(DC):
            p = pspool[2 + (hc % 2)]
            W(nc.tensor, ("psfree", p.name))
            for kc in range(KC):
                mm = nc.tensor.matmul(
                    p[:, 0:QH],
                    vt[:, kc * D + hc * 128: kc * D + (hc + 1) * 128],
                    probsT[:, kc * QH:(kc + 1) * QH],
                    start=(kc == 0), stop=(kc == KC - 1))
            tcx.inc(mm, 1)
            nc.vector.wait_ge(tcx.h, tcx.n)
            vc.inc(nc.vector.tensor_copy(
                ctxf[:, hc * QH:(hc + 1) * QH], p[:, 0:QH]), 1)
            st[("psfree", p.name)] = (vc, vc.n)
        st[("ctx_done",)] = (vc, vc.n)
        st[("probsT_done",)] = (tcx, tcx.n)
        nc.tensor.wait_ge(s_wol.h, s_wol.n)
        W(nc.tensor, ("ctx_done",))
        for qc2 in range(2):
            gt = 2 * h + qc2
            asl = 0
            for dh in range(DHN):
                p = pspool[dh % 2]
                W(nc.tensor, ("psfree", p.name))
                for hc in range(DC):
                    mm = nc.tensor.matmul(
                        p[:, 0:DW],
                        ctxf[:, hc * QH + qc2 * 128: hc * QH + (qc2 + 1) * 128],
                        wot[:, hc * D + dh * DW: hc * D + (dh + 1) * DW],
                        start=(hc == 0), stop=(hc == DC - 1))
                tcx.inc(mm, 1)
                nc.vector.wait_ge(tcx.h, tcx.n)
                W(nc.vector, ("aot_used", asl))
                vc.inc(nc.vector.tensor_scalar_mul(
                    aot[asl][:, 0:DW], p[:, 0:DW],
                    sums[:, gt:gt + 1]), 1)
                st[("psfree", p.name)] = (vc, vc.n)
                nc.sync.wait_ge(vc.h, vc.n)
                s_aos[asl].dma(nc.sync.dma_start(
                    out=aosp_d[gt * 128:(gt + 1) * 128, dh * DW:(dh + 1) * DW],
                    in_=aot[asl][:, 0:DW]))
                st[("aot_used", asl)] = (s_aos[asl], s_aos[asl].n)

    def emit_qld(gt):
        qs = gt % 2
        W(nc.sync, ("qtile_used", qs))
        if gt < 2:
            nc.sync.wait_ge(s_qst[0].h, s_qst[0].n)
            nc.sync.wait_ge(s_qst[1].h, s_qst[1].n)
            nc.sync.wait_ge(vc.h, st[('P2_VC',)])
            nc.sync.wait_ge(tcx.h, P1_TC)
        s_qsl[qs].dma(nc.sync.dma_start(
            out=qtile[qs][:, :].rearrange("p (c q) -> p c q", c=DC),
            in_=bass.AP(qsp_d, gt * 128, [[TOK, 128], [128 * TOK, DC], [1, 128]])))
        st[("qldt", gt)] = (s_qsl[qs], s_qsl[qs].n)

    NGT = 4 * NQB
    prev = None
    for gt in range(NGT):
        if gt == 0:
            emit_qld(0)
            emit_qld(1)
        emit_scores(gt, gt % 2)
        if gt + 2 < NGT:
            emit_qld(gt + 2)
        if prev is not None:
            emit_transposes(prev)
        emit_softmax(gt)
        if prev is not None and prev % 2 == 1:
            emit_ctx_ao(prev // 2)
        prev = gt
    emit_transposes(prev)
    emit_ctx_ao(prev // 2)
    P3_TC = tcx.n
    P3_VC = vc.n
    P3_AC = acx.n
    stkB.close()

    gidx = SBC("gidx_sb", [128, E * GIW], I16)
    dispb = [SBC(f"dispb{i}", [128, DC * PAD], BF16) for i in range(2)]
    f1sl = [SBC(f"f1sl{i}", [128, DC * HS], BF16) for i in range(3)]
    f2sb = SBC("f2sb", [128, HC * D], BF16)
    hid = SBC("hid", [128, HC * PAD], BF16)
    f1bs = SBC("f1bs", [128, HC], F32)
    f2br = SBC("f2br", [1, D], F32)
    b2bc = SBC("b2bc", [128, D], F32)
    onesr4 = SBC("onesr4", [1, 128], F32)
    ytile = [SBC(f"ytile{i}", [128, D], BF16) for i in range(2)]

    for eng in [nc.gpsimd, nc.sync]:
        eng.wait_ge(tcx.h, P3_TC)
        eng.wait_ge(vc.h, P3_VC)
        eng.wait_ge(acx.h, P3_AC)
        eng.wait_ge(s_aos[0].h, s_aos[0].n)
        eng.wait_ge(s_aos[1].h, s_aos[1].n)
        eng.wait_ge(SCAT[0][0].h, SCAT[0][1])
        eng.wait_ge(SCAT[1][0].h, SCAT[1][1])
    s_cst.dma(nc.sync.dma_start(out=gidx[:, :], in_=gidx_h[:, :]))
    SQ_CONST = s_cst.n
    nc.gpsimd.wait_ge(s_cst.h, SQ_CONST)
    nc.gpsimd.wait_ge(SCAT[0][0].h, SCAT[0][1])
    nc.gpsimd.wait_ge(SCAT[1][0].h, SCAT[1][1])
    nc.vector.wait_ge(tcx.h, P3_TC)
    vc.inc(nc.vector.memset(onesr4[:, :], 1.0), 1)
    ONESR4_OK = vc.n
    nc.tensor.wait_ge(vc.h, ONESR4_OK)
    for e in range(E):
        ds = e % 2
        W(nc.gpsimd, ("disp_used", ds))
        g_gth[ds].dma(nc.gpsimd.dma_gather(
            dispb[ds][:, :].rearrange("p (c s) -> p c s", c=DC),
            disp_d[:, :], gidx[:, e * GIW:(e + 1) * GIW], PAD, PAD, D,
            transpose=True))
        st[("disp", e)] = (g_gth[ds], g_gth[ds].n)
        W(nc.sync, ("f1b_used",))
        W(nc.sync, ("f2b_used",))
        s_bld.dma(nc.sync.dma_start(
            out=f1bs[:, :], in_=bass.AP(fc1b_h, e * H, [[1, 128], [128, HC]]),
            allow_slow_non_contiguous=True))
        s_bld.dma(nc.sync.dma_start(out=f2br[:, :], in_=fc2b_h[e:e + 1, :]))
        st[("bld", e)] = (s_bld, s_bld.n)
        for hs in range(NHS):
            fs = (e * NHS + hs) % 3
            W(nc.gpsimd, ("f1sl_used", fs))
            g_f1l[fs].dma(nc.gpsimd.dma_start(
                out=f1sl[fs][:, :],
                in_=fc1s_h[(e * NHS + hs) * 128:(e * NHS + hs + 1) * 128, :]))
            st[("f1sl", e, hs)] = (g_f1l[fs], g_f1l[fs].n)
            HQ = HC // 4
            for q_ in range(4):
                if hs == min(1 + q_, NHS - 1) and ("f2q_em", e, q_) not in st:
                    st[("f2q_em", e, q_)] = True
                    qsem = g_f2q[q_]
                    W(nc.gpsimd, ("f2_used",))
                    qsem.dma(nc.gpsimd.dma_start(
                        out=f2sb[:, q_ * HQ * D:(q_ + 1) * HQ * D],
                        in_=fc2r_h[e * 128:(e + 1) * 128,
                                   q_ * HQ * D:(q_ + 1) * HQ * D]))
                    st[("f2q", e, q_)] = (qsem, qsem.n)
            W(nc.tensor, ("f1sl", e, hs))
            if hs == 0:
                W(nc.tensor, ("disp", e))
            for hcc in range(HSC):
                hhc = hs * HSC + hcc
                for sbk in range(NSB):
                    p = pspool[(hcc * NSB + sbk) % 4]
                    W(nc.tensor, ("psfree", p.name))
                    for dc in range(DC):
                        mm = nc.tensor.matmul(
                            p[:, 0:SBW],
                            f1sl[fs][:, dc * HS + hcc * 128: dc * HS + (hcc + 1) * 128],
                            dispb[ds][:, dc * PAD + sbk * SBW:
                                      dc * PAD + (sbk + 1) * SBW],
                            start=(dc == 0), stop=(dc == DC - 1))
                    tcx.inc(mm, 1)
                    nc.scalar.wait_ge(tcx.h, tcx.n)
                    W(nc.scalar, ("bld", e))
                    acx.inc(nc.scalar.activation(
                        hid[:, hhc * PAD + sbk * SBW: hhc * PAD + (sbk + 1) * SBW],
                        p[:, 0:SBW], AF.Silu, bias=f1bs[:, hhc:hhc + 1]), 1)
                    st[("psfree", p.name)] = (acx, acx.n)
            st[("f1sl_used", fs)] = (acx, acx.n)
        st[("disp_used", ds)] = (tcx, tcx.n)
        st[("f1b_used",)] = (acx, acx.n)
        HID_OK = acx.n


        nc.tensor.wait_ge(acx.h, HID_OK)
        W(nc.tensor, ("bld", e))
        p = pspool[2]
        for dh in range(DHN):
            W(nc.tensor, ("psfree", p.name))
            mm = nc.tensor.matmul(p[:, 0:DW], onesr4[0:1, 0:128],
                                  f2br[:, dh * DW:(dh + 1) * DW],
                                  start=True, stop=True)
            tcx.inc(mm, 1)
            nc.vector.wait_ge(tcx.h, tcx.n)
            W(nc.vector, ("b2bc_used",))
            vc.inc(nc.vector.tensor_copy(b2bc[:, dh * DW:(dh + 1) * DW],
                                         p[:, 0:DW]), 1)
            st[("psfree", p.name)] = (vc, vc.n)
        st[("b2bc",)] = (vc, vc.n)
        W(nc.vector, ("b2bc",))
        for sc in range(NSL):
            ys = sc % 2
            for dh in range(DHN):
                p = pspool[dh % 2]
                W(nc.tensor, ("psfree", p.name))
                for hhc in range(HC):
                    if sc == 0 and dh == 0 and hhc % (HC // 4) == 0:
                        W(nc.tensor, ("f2q", e, hhc // (HC // 4)))
                    mm = nc.tensor.matmul(
                        p[:, 0:DW],
                        hid[:, hhc * PAD + sc * 128: hhc * PAD + (sc + 1) * 128],
                        f2sb[:, hhc * D + dh * DW: hhc * D + (dh + 1) * DW],
                        start=(hhc == 0), stop=(hhc == HC - 1))
                tcx.inc(mm, 1)
                nc.vector.wait_ge(tcx.h, tcx.n)
                W(nc.vector, ("ytile_used", ys))
                vc.inc(nc.vector.tensor_tensor(
                    ytile[ys][:, dh * DW:(dh + 1) * DW], p[:, 0:DW],
                    b2bc[:, dh * DW:(dh + 1) * DW], op=OP.add), 1)
                st[("psfree", p.name)] = (vc, vc.n)
            st[("yev", e, sc)] = (vc, vc.n)
            W(nc.sync, ("yev", e, sc))
            s_yst[ys].dma(nc.sync.dma_start(
                out=ybuf_d[e * PAD + sc * 128: e * PAD + (sc + 1) * 128, :],
                in_=ytile[ys][:, :]))
            st[("ytile_used", ys)] = (s_yst[ys], s_yst[ys].n)
        st[("f2_used",)] = (tcx, tcx.n)
        st[("f2b_used",)] = (tcx, tcx.n)
        st[("b2bc_used",)] = (tcx, tcx.n)
    YST = [(s_yst[0], s_yst[0].n), (s_yst[1], s_yst[1].n)]
    P4_TC = tcx.n
    P4_VC = vc.n
    P4_AC = acx.n
    stkC.close()

    lngb = SB("lngb", [128, D], BF16)
    lnbb = SB("lnbb", [128, D], BF16)
    yg = [SB(f"yg{i}", [128, 2 * D], BF16) for i in range(2)]
    xno = [SB(f"xno{i}", [128, D], BF16) for i in range(2)]
    xrl = [SB(f"xrl{i}", [128, D], F32) for i in range(2)]
    aorl = [SB(f"aorl{i}", [128, D], BF16) for i in range(2)]
    rest_ = [SB(f"rest{i}", [128, D], F32) for i in range(2)]
    outt = [SB(f"outt{i}", [128, D], F32) for i in range(2)]

    for eng in [nc.gpsimd, nc.sync]:
        eng.wait_ge(YST[0][0].h, YST[0][1])
        eng.wait_ge(YST[1][0].h, YST[1][1])
        eng.wait_ge(tcx.h, P4_TC)
        eng.wait_ge(vc.h, P4_VC)
        eng.wait_ge(acx.h, P4_AC)
        for q_ in range(4):
            eng.wait_ge(g_f2q[q_].h, g_f2q[q_].n)
        eng.wait_ge(g_gth[0].h, g_gth[0].n)
        eng.wait_ge(g_gth[1].h, g_gth[1].n)
        for q_ in range(3):
            eng.wait_ge(g_f1l[q_].h, g_f1l[q_].n)
        eng.wait_ge(s_bld.h, s_bld.n)
    g_cst.dma(nc.gpsimd.dma_start(
        out=lngb[:, :], in_=bass.AP(lng_h, 0, [[0, 128], [1, D]])))
    g_cst.dma(nc.gpsimd.dma_start(
        out=lnbb[:, :], in_=bass.AP(lnb_h, 0, [[0, 128], [1, D]])))
    nc.sync.wait_ge(s_aos[0].h, s_aos[0].n)
    nc.sync.wait_ge(s_aos[1].h, s_aos[1].n)
    nc.sync.wait_ge(s_xnw[0].h, s_xnw[0].n)
    nc.sync.wait_ge(s_xnw[1].h, s_xnw[1].n)
    for t in range(NT):
        ts = t % 2
        W(nc.gpsimd, ("yg_used", ts))
        for j in range(2):
            g_ygl[ts].dma(nc.gpsimd.indirect_dma_start(
                out=yg[ts][:, j * D:(j + 1) * D], out_offset=None,
                in_=ybuf_d[:, :],
                in_offset=bass.IndirectOffsetOnAxis(
                    ap=slots[:, t * 2 + j: t * 2 + j + 1], axis=0)))
        st[("yg", t)] = (g_ygl[ts], g_ygl[ts].n)
        W(nc.sync, ("xno_used", ts))
        s_x5a[ts].dma(nc.sync.dma_start(out=xno[ts][:, :],
                                        in_=xnsp_d[t * 128:(t + 1) * 128, :]))
        st[("xno", t)] = (s_x5a[ts], s_x5a[ts].n)
        W(nc.sync, ("xrl_used", ts))
        s_xrl[ts].dma(nc.sync.dma_start(out=xrl[ts][:, :],
                                        in_=xo_h[t * 128:(t + 1) * 128, :]))
        st[("xrl", t)] = (s_xrl[ts], s_xrl[ts].n)
        W(nc.sync, ("aorl_used", ts))
        s_x5c[ts].dma(nc.sync.dma_start(out=aorl[ts][:, :],
                                        in_=aosp_d[t * 128:(t + 1) * 128, :]))
        st[("aorl", t)] = (s_x5c[ts], s_x5c[ts].n)

        W(nc.vector, ("yg", t))
        W(nc.vector, ("xno", t))
        W(nc.vector, ("xrl", t))
        W(nc.vector, ("aorl", t))
        vc.inc(nc.vector.tensor_scalar_mul(rest_[ts][:, :], yg[ts][:, 0:D],
                                           gates[:, t * 2: t * 2 + 1]), 1)
        nc.vector.wait_ge(vc.h, vc.n)
        vc.inc(nc.vector.scalar_tensor_tensor(
            rest_[ts][:, :], yg[ts][:, D:2 * D], gates[:, t * 2 + 1: t * 2 + 2],
            rest_[ts][:, :], op0=OP.mult, op1=OP.add), 1)
        nc.vector.wait_ge(vc.h, vc.n)
        st[("yg_used", ts)] = (vc, vc.n)
        vc.inc(nc.vector.tensor_add(rest_[ts][:, :], rest_[ts][:, :], xno[ts][:, :]), 1)
        st[("xno_used", ts)] = (vc, vc.n)
        nc.vector.wait_ge(vc.h, vc.n)
        nst = (D + 511) // 512
        for c_ in range(nst):
            w_ = min(512, D - c_ * 512)
            vc.inc(nc.vector.bn_stats(mvst[:, 6 * c_:6 * c_ + 6],
                                      rest_[ts][:, c_ * 512: c_ * 512 + w_]), 1)
            nc.vector.wait_ge(vc.h, vc.n)
        vc.inc(nc.vector.bn_aggr(mvst[:, 12:14], mvst[:, 0:6 * nst]), 1)
        nc.vector.wait_ge(vc.h, vc.n)
        vc.inc(nc.vector.tensor_scalar_add(mvst[:, 14:15], mvst[:, 13:14], 1e-5), 1)
        nc.scalar.wait_ge(vc.h, vc.n)
        acx.inc(nc.scalar.activation(mvst[:, 15:16], mvst[:, 14:15], AF.Sqrt), 1)
        nc.vector.wait_ge(acx.h, acx.n)
        vc.inc(nc.vector.reciprocal(mvst[:, 15:16], mvst[:, 15:16]), 1)
        nc.vector.wait_ge(vc.h, vc.n)
        os_ = t % 2
        W(nc.vector, ("outt_used", os_))
        vc.inc(nc.vector.tensor_scalar(outt[os_][:, :], rest_[ts][:, :], mvst[:, 12:13],
                                       mvst[:, 15:16], op0=OP.subtract, op1=OP.mult), 1)
        nc.vector.wait_ge(vc.h, vc.n)
        nc.vector.wait_ge(g_cst.h, g_cst.n)
        vc.inc(nc.vector.tensor_mul(outt[os_][:, :], outt[os_][:, :], lngb[:, :]), 1)
        nc.vector.wait_ge(vc.h, vc.n)
        vc.inc(nc.vector.tensor_add(outt[os_][:, :], outt[os_][:, :], lnbb[:, :]), 1)
        nc.vector.wait_ge(vc.h, vc.n)
        vc.inc(nc.vector.tensor_add(outt[os_][:, :], outt[os_][:, :],
                                    xrl[ts][:, :]), 1)
        st[("xrl_used", ts)] = (vc, vc.n)
        nc.vector.wait_ge(vc.h, vc.n)
        vc.inc(nc.vector.tensor_add(outt[os_][:, :], outt[os_][:, :],
                                    aorl[ts][:, :]), 1)
        st[("aorl_used", ts)] = (vc, vc.n)
        st[("outv", t)] = (vc, vc.n)
        W(nc.sync, ("outv", t))
        s_ost[os_].dma(nc.sync.dma_start(out=out_h[t * 128:(t + 1) * 128, :],
                                         in_=outt[os_][:, :]))
        st[("outt_used", os_)] = (s_ost[os_], s_ost[os_].n)

    nc.sync.wait_ge(s_ost[0].h, s_ost[0].n)
    nc.sync.wait_ge(s_ost[1].h, s_ost[1].n)
    stk.close()
    return nc


def _host_prep(cfg, x_own, x_oth, W_qkv, W_out, wg, fc1_w, fc1_b, fc2_w, fc2_b,
               ln_g, ln_b, core_id):
    D = cfg["D"]; E = cfg["E"]; PAD = cfg["PAD"]; NC_ = cfg["NCORES"]
    GIW = PAD // 16
    f32 = np.float32
    Wq = np.ascontiguousarray(W_qkv[0::3, :].T, dtype=f32)
    Wk = np.ascontiguousarray(W_qkv[1::3, :].T, dtype=f32)
    Wv = np.ascontiguousarray(W_qkv[2::3, :].T, dtype=f32)
    Wo = np.ascontiguousarray(W_out.T, dtype=f32)
    E_, H_, D_ = fc1_w.shape
    DCh, HCh = D_ // 128, H_ // 128
    HSh = min(H_, 1024); NHSh = H_ // HSh
    fc1s = np.ascontiguousarray(
        fc1_w.transpose(0, 2, 1).reshape(E_, DCh, 128, NHSh, HSh)
        .transpose(0, 3, 2, 1, 4).reshape(E_ * NHSh * 128, DCh * HSh),
        dtype=f32)
    fc2r = np.ascontiguousarray(
        fc2_w.reshape(E_, HCh, 128, D_).transpose(0, 2, 1, 3)
        .reshape(E_ * 128, HCh * D_), dtype=f32)
    tri = (np.arange(128)[None, :] > np.arange(128)[:, None]).astype(f32)
    ident = np.eye(128, dtype=f32)
    e8 = np.broadcast_to(np.arange(E, dtype=f32)[None, :], (128, E)).copy()
    epad = np.broadcast_to(
        np.array([e * PAD for j in range(2) for e in range(E)], f32)[None, :],
        (128, 2 * E)).copy()
    pmask = np.zeros((1, 16 * NC_), f32)
    pmask[0, :16 * core_id] = 1.0
    gi = (np.arange(E * GIW, dtype=np.int64)[None, :] * 16
          + (np.arange(128) % 16)[:, None]).astype(np.int16)
    return {
        "x_own": np.ascontiguousarray(x_own, dtype=f32),
        "x_oth": np.ascontiguousarray(x_oth, dtype=f32),
        "wq_t": Wq, "wk_t": Wk, "wv_t": Wv, "wo_t": Wo,
        "wg": np.ascontiguousarray(wg, dtype=f32),
        "fc1s": fc1s, "fc2r": fc2r,
        "fc1b": np.ascontiguousarray(fc1_b, dtype=f32),
        "fc2b": np.ascontiguousarray(fc2_b, dtype=f32),
        "ln_g": np.ascontiguousarray(ln_g.reshape(1, D), dtype=f32),
        "ln_b": np.ascontiguousarray(ln_b.reshape(1, D), dtype=f32),
        "tri": tri, "ident": ident, "e8": e8, "epad_bc": epad,
        "pmask": pmask, "gidx": gi,
    }


def finalize_for_run(nc):
    mask = {}
    for lib in all_libraries:
        for it in lib.instructions:
            mask[it] = mask.get(it, 0) | (1 << lib.index)
    _bass_rust.insert_library_loads(nc, mask, len(all_libraries), _std_lib.index)
    mybir.codegen_inst_isa_subclasses(nc)


def kernel(L0_attn_x, L0_attn_W_qkv, L0_attn_W_out, L0_ffn_wg, L0_ffn_fc1_w,
           L0_ffn_fc1_b, L0_ffn_fc2_w, L0_ffn_fc2_b, L0_ffn_ln_gamma,
           L0_ffn_ln_beta):
    cfg = dict(FULL_CFG); cfg['p2_early'] = False
    B, S, D = 4, 4096, 1024
    TOK = cfg["TOK"]
    x = np.asarray(L0_attn_x, np.float32)
    nc = build_program(cfg)
    finalize_for_run(nc)
    in_maps = []
    for c in range(8):
        b, h = c // 2, c % 2
        x_own = x[b, h * TOK:(h + 1) * TOK, :]
        x_oth = x[b, (1 - h) * TOK:(2 - h) * TOK, :]
        in_maps.append(_host_prep(
            cfg, x_own, x_oth, np.asarray(L0_attn_W_qkv), np.asarray(L0_attn_W_out),
            np.asarray(L0_ffn_wg), np.asarray(L0_ffn_fc1_w), np.asarray(L0_ffn_fc1_b),
            np.asarray(L0_ffn_fc2_w), np.asarray(L0_ffn_fc2_b),
            np.asarray(L0_ffn_ln_gamma), np.asarray(L0_ffn_ln_beta), c))
    res = run_bass_kernel_spmd(nc, in_maps, core_ids=list(range(8)))
    outs = [np.asarray(r["out"]) for r in res.results]
    return np.concatenate(outs, 0).reshape(B, S, D).astype(np.float32)


# version 40 (final)
# speedup vs baseline: 1.0506x; 1.0506x over previous
import math
import numpy as np
import ml_dtypes
from contextlib import ExitStack
import concourse.bass as bass
import concourse.mybir as mybir
import bass_rust as _bass_rust
from concourse.bass_utils import run_bass_kernel_spmd
from concourse.library_config import all_libraries, standard as _std_lib

F32 = mybir.dt.float32
BF16 = mybir.dt.bfloat16
U32 = mybir.dt.uint32
I16 = mybir.dt.int16
AX = mybir.AxisListType.X
AF = mybir.ActivationFunctionType
OP = mybir.AluOpType

FULL_CFG = dict(D=1024, E=8, H=4096, TOK=2048, SF=4096, PAD=640, CAP=5120,
                NCORES=8)


def build_program(cfg):
    D = cfg["D"]; E = cfg["E"]; H = cfg["H"]; TOK = cfg["TOK"]
    SF = cfg["SF"]; PAD = cfg["PAD"]; CAP = cfg["CAP"]; NCORES = cfg["NCORES"]
    assert E == 8
    DC = D // 128; HC = H // 128; NT = TOK // 128
    NBF = SF // 512; NOB = TOK // 512
    EPAD = E * PAD; NSL = PAD // 128; GIW = PAD // 16
    NQB = TOK // 512
    KC = SF // 128
    SCALE = 1.0 / math.sqrt(D)
    DW = min(D, 512); DHN = D // DW
    QH = 256
    SBW = PAD // 2 if PAD // 2 <= 512 else 512
    NSB = PAD // SBW
    HS = min(H, 1024); NHS = H // HS
    HSC = HS // 128

    nc = bass.Bass("TRN2", target_bir_lowering=False, debug=False)

    xo_h = nc.dram_tensor("x_own", [TOK, D], F32, kind="ExternalInput")
    xt_h = nc.dram_tensor("x_oth", [SF - TOK, D], F32, kind="ExternalInput")
    wqt_h = nc.dram_tensor("wq_t", [D, D], F32, kind="ExternalInput")
    wkt_h = nc.dram_tensor("wk_t", [D, D], F32, kind="ExternalInput")
    wvt_h = nc.dram_tensor("wv_t", [D, D], F32, kind="ExternalInput")
    wot_h = nc.dram_tensor("wo_t", [D, D], F32, kind="ExternalInput")
    wg_h = nc.dram_tensor("wg", [D, E], F32, kind="ExternalInput")
    fc1s_h = nc.dram_tensor("fc1s", [E * (H // min(H, 1024)) * 128,
                                     (D // 128) * min(H, 1024)], F32,
                            kind="ExternalInput")
    fc2r_h = nc.dram_tensor("fc2r", [E * 128, (H // 128) * D], F32,
                            kind="ExternalInput")
    fc1b_h = nc.dram_tensor("fc1b", [E, H], F32, kind="ExternalInput")
    fc2b_h = nc.dram_tensor("fc2b", [E, D], F32, kind="ExternalInput")
    lng_h = nc.dram_tensor("ln_g", [1, D], F32, kind="ExternalInput")
    lnb_h = nc.dram_tensor("ln_b", [1, D], F32, kind="ExternalInput")
    tri_h = nc.dram_tensor("tri", [128, 128], F32, kind="ExternalInput")
    ident_h = nc.dram_tensor("ident", [128, 128], F32, kind="ExternalInput")
    e8_h = nc.dram_tensor("e8", [128, E], F32, kind="ExternalInput")
    epad_h = nc.dram_tensor("epad_bc", [128, 2 * E], F32, kind="ExternalInput")
    pmask_h = nc.dram_tensor("pmask", [1, 16 * NCORES], F32, kind="ExternalInput")
    gidx_h = nc.dram_tensor("gidx", [128, E * GIW], I16, kind="ExternalInput")
    out_h = nc.dram_tensor("out", [TOK, D], F32, kind="ExternalOutput")

    disp_d = nc.dram_tensor("disp_d", [EPAD, D], BF16)
    ybuf_d = nc.dram_tensor("ybuf_d", [EPAD, D], BF16)
    xnsp_d = nc.dram_tensor("xnsp_d", [TOK, D], BF16)
    qsp_d = nc.dram_tensor("qsp_d", [DC, 128, TOK], BF16)
    aosp_d = nc.dram_tensor("aosp_d", [TOK, D], BF16)
    cci_d = nc.dram_tensor("cci_d", [1, 2 * E], F32)
    cco_d = nc.dram_tensor("cco_d", [NCORES, 2 * E], F32)

    stk = ExitStack()
    stkA = ExitStack()
    stkB = ExitStack()
    stkC = ExitStack()
    SB = lambda n, s, d: stk.enter_context(nc.sbuf_tensor(n, s, d))
    SBA = lambda n, s, d: stkA.enter_context(nc.sbuf_tensor(n, s, d))
    SBB = lambda n, s, d: stkB.enter_context(nc.sbuf_tensor(n, s, d))
    SBC = lambda n, s, d: stkC.enter_context(nc.sbuf_tensor(n, s, d))
    PS = lambda n, s, d: stk.enter_context(nc.psum_tensor(n, s, d))

    class Sem:

        def __init__(self, name):
            self.h = stk.enter_context(nc.semaphore(name)); self.n = 0

        def inc(self, instr, k=1):
            instr.then_inc(self.h, k); self.n += k; return self.n

        def dma(self, instr):
            return self.inc(instr, 16)

    vc = Sem("vc"); acx = Sem("acx"); tcx = Sem("tcx"); gcc = Sem("gcc")
    ccs = Sem("ccs")
    s_cst = Sem("s_cst"); s_pre = Sem("s_pre"); s_ccl = Sem("s_ccl")
    s_wol = Sem("s_wol")
    s_bld = Sem("s_bld")
    s_xld = [Sem(f"s_xld{i}") for i in range(2)]
    s_xnw = [Sem(f"s_xnw{i}") for i in range(2)]
    s_qst = [Sem(f"s_qst{i}") for i in range(2)]
    s_qsl = [Sem(f"s_qsl{i}") for i in range(2)]
    s_aos = [Sem(f"s_aos{i}") for i in range(2)]
    s_yst = [Sem(f"s_yst{i}") for i in range(2)]
    s_x5a = [Sem(f"s_x5a{i}") for i in range(2)]
    s_x5c = [Sem(f"s_x5c{i}") for i in range(2)]
    s_ost = [Sem(f"s_ost{i}") for i in range(2)]
    s_xrl = [Sem(f"s_xrl{i}") for i in range(2)]
    s_xsc = [Sem(f"s_xsc{i}") for i in range(2)]
    g_cst = Sem("g_cst"); g_cci = Sem("g_cci")
    g_w3p = [Sem(f"g_w3p{i}") for i in range(4)]
    g_f2q = [Sem(f"g_f2q{i}") for i in range(4)]
    g_sct = [Sem(f"g_sct{i}") for i in range(2)]
    g_xsl = Sem("g_xsl")
    g_xld = [Sem(f"g_xld{i}") for i in range(2)]
    g_gth = [Sem(f"g_gth{i}") for i in range(2)]
    g_f1l = [Sem(f"g_f1l{i}") for i in range(2)]
    g_ygl = [Sem(f"g_ygl{i}") for i in range(2)]

    ident = SB("ident_sb", [128, 128], BF16)
    onesc = SB("onesc", [128, 1], BF16)
    gates = SB("gates_sb", [128, NT * 2], F32)
    slots = SB("slots_sb", [128, NT * 2], U32)
    sums = SB("sums_sb", [128, NT], F32)
    mvst = SB("mvst", [128, 16], F32)
    xns = [SBB("xns0", [128, D], BF16)]
    kf = SBB("kf", [128, DC * SF], BF16)
    vt = SBB("vt", [128, KC * D], BF16)
    onesr = SBA("onesr", [1, 128], F32)
    tri = SBA("tri_sb", [128, 128], BF16)
    e8 = SBA("e8_sb", [128, E], F32)
    epad = SBA("epad_sb", [128, 2 * E], F32)
    pmask = SBA("pmask_sb", [1, 16 * NCORES], F32)
    g0buf = SBA("g0buf", [128, NT * 2], F32)
    masksf = SBA("masksf", [128, NT * 16], BF16)
    totrow = SBA("totrow", [1, 16 * (NT + 1)], F32)
    offrow = SBA("offrow", [1, 16 * NT], F32)
    rows48 = SBA("rows48", [1, 48 * NT], F32)
    cntloc = SBA("cntloc", [1, 16], F32)
    ccrow = SBA("ccrow", [1, 16 * NCORES], F32)
    masked = SBA("maskedcc", [1, 16 * NCORES], F32)
    rowa = SBA("rowa", [1, 16], F32)
    logt = SBA("logt", [128, NT * E], BF16)
    top8 = SBA("top8_sb", [128, 8], F32)
    topi = SBA("topi_sb", [128, 8], U32)
    rkA = SBA("rkA", [128, (NT // 4) * 48], F32)
    tA = SBA("tA", [128, (NT // 4) * 16], F32)
    k2a = SBA("k2a", [128, NT * 2], F32)
    idxall = SBA("idxall", [128, 2 * NT], F32)
    g0a = SBA("g0a", [128, NT], F32)
    g0s = SBA("g0s", [128, NT], F32)

    ps0 = PS("ps0", [128, 512], F32)
    ps1 = PS("ps1", [128, 512], F32)
    ps2 = PS("ps2", [128, 512], F32)
    ps3 = PS("ps3", [128, 512], F32)
    pst0 = PS("pst0", [128, 1024], BF16)
    pst1 = PS("pst1", [128, 1024], BF16)
    psr = PS("psr", [128, 512], F32)
    psw = PS("psw", [128, 512], F32)
    pspool = [ps0, ps1, ps2, ps3]

    st = {}

    def W(eng, ev, default=None):
        v = st.get(ev, default)
        if v is None:
            return
        eng.wait_ge(v[0].h, v[1])

    zsrc = kf[:, DC * SF - 512: DC * SF]
    nc.gpsimd.memset(zsrc, 0)
    gcc.inc(nc.gpsimd.engine_nop(), 1)
    g_cst.dma(nc.gpsimd.dma_start(out=tri[:, :], in_=tri_h[:, :]))
    g_cst.dma(nc.gpsimd.dma_start(out=ident[:, :], in_=ident_h[:, :]))
    st[("ident",)] = (g_cst, g_cst.n)

    s_cst.dma(nc.sync.dma_start(out=e8[:, :], in_=e8_h[:, :]))
    s_cst.dma(nc.sync.dma_start(out=epad[:, :], in_=epad_h[:, :]))
    s_cst.dma(nc.sync.dma_start(out=pmask[:, :], in_=pmask_h[:, :]))
    SQ_CONST = s_cst.n

    nc.vector.memset(onesr[:, :], 1.0)
    nc.vector.memset(onesc[:, :], 1.0)
    nc.vector.memset(totrow[:, :], 0.0)
    nc.vector.memset(offrow[:, :], 0.0)
    vc.inc(nc.vector.memset(rows48[:, :], 0.0), 1)
    VC_CONST = vc.n

    nc.sync.wait_ge(gcc.h, 1)
    pre_jobs = [(t, hv) for t in range(EPAD // 128)
                for hv in range((D + 511) // 512)]

    def emit_prefill_chunk(k, nchunks):
        per = -(-len(pre_jobs) // nchunks)
        for t, hv in pre_jobs[k * per:(k + 1) * per]:
            w_ = min(512, D - hv * 512)
            s_pre.dma(nc.sync.dma_start(
                out=disp_d[t * 128:(t + 1) * 128, hv * 512:hv * 512 + w_],
                in_=zsrc[:, 0:w_]))

    w3t = SBA("w3t", [128, 3 * DC * D], BF16)
    wgsb = SBA("wgsb", [128, DC * E], BF16)
    xtile = [SBA(f"xtile{i}", [128, D], BF16) for i in range(2)]
    xnf = [SBA("xnf0", [128, DC * 512], BF16)]
    xnt = [SBA(f"xnt{i}", [128, D], BF16) for i in range(2)]
    qevb = [SBA("qevb0", [128, 512], BF16), None]
    qevb1 = SBA("qevb1", [128, 512], BF16)
    logp = SBA("logp", [128, 512], BF16)

    w3_pieces = [(1, 0), (1, 1), (0, 0), (0, 1), (3, 0), (2, 0), (2, 1)]
    w3_keys = {1: ("w3k",), 0: ("w3q",), 2: ("w3v",)}

    def emit_w3_piece(k):
        if k >= len(w3_pieces):
            return
        wi, hh = w3_pieces[k]
        sem = g_w3p[wi]
        if wi == 3:
            sem.dma(nc.gpsimd.dma_start(
                out=wgsb[:, :].rearrange("p (c e) -> p c e", c=DC),
                in_=bass.AP(wg_h, 0, [[E, 128], [128 * E, DC], [1, E]])))
            st[("wg",)] = (sem, sem.n)
            return
        wh = [wqt_h, wkt_h, wvt_h][wi]
        DH2 = DC // 2
        sem.dma(nc.gpsimd.dma_start(
            out=w3t[:, wi * DC * D + hh * DH2 * D:
                    wi * DC * D + (hh + 1) * DH2 * D].rearrange(
                "p (c o) -> p c o", c=DH2),
            in_=bass.AP(wh, hh * DH2 * 128 * D, [[D, 128], [128 * D, DH2], [1, D]])))
        if hh == 1:
            st[w3_keys[wi]] = (sem, sem.n)

    blocks = [(xo_h, tb * 512, True) if tb < NOB else
              (xt_h, (tb - NOB) * 512, False) for tb in range(NBF)]

    def emit_p2a1(lo=0, hi=None):
        hi = NT if hi is None else hi
        if lo == 0:
            nc.vector.wait_ge(s_cst.h, SQ_CONST)
        for t in range(lo, hi):
            W(nc.vector, ("logt", t))
            vc.inc(nc.vector.max(top8[:, :], logt[:, t * E:(t + 1) * E]), 1)
            nc.vector.wait_ge(vc.h, vc.n)
            vc.inc(nc.vector.max_index(topi[:, :], top8[:, :],
                                       logt[:, t * E:(t + 1) * E]), 1)
            nc.vector.wait_ge(vc.h, vc.n)
            vc.inc(nc.vector.tensor_copy(idxall[:, 2 * t:2 * t + 2],
                                         topi[:, 0:2]), 1)
            vc.inc(nc.vector.tensor_sub(g0a[:, t:t + 1], top8[:, 0:1],
                                        top8[:, 1:2]), 1)
            nc.vector.wait_ge(vc.h, vc.n)
        st[('p2a1_hi',)] = hi

    def emit_p2a1b():
        nc.scalar.wait_ge(vc.h, vc.n)
        acx.inc(nc.scalar.activation(g0s[:, :], g0a[:, :], AF.Sigmoid), 1)
        nc.vector.wait_ge(acx.h, acx.n)
        vc.inc(nc.vector.tensor_copy(
            bass.AP(g0buf, 0, [[2 * NT, 128], [2, NT]]), g0s[:, :]), 1)
        vc.inc(nc.vector.tensor_scalar(
            bass.AP(g0buf, 1, [[2 * NT, 128], [2, NT]]), g0s[:, :], -1.0, 1.0,
            op0=OP.mult, op1=OP.add), 1)
        nc.vector.wait_ge(vc.h, vc.n)
        for t in range(NT):
            for j in range(2):
                vc.inc(nc.vector.tensor_scalar(
                    masksf[:, t * 16 + j * 8: t * 16 + (j + 1) * 8], e8[:, :],
                    idxall[:, 2 * t + j:2 * t + j + 1], None, op0=OP.is_equal), 1)
            st[("mask", t)] = (vc, vc.n)
        st[('p2a1b_done',)] = True

    def emit_p2a2():
        for t in range(NT):
            W(nc.tensor, ("mask", t))
            W(nc.tensor, ("pswfree2",), st.get(("pswfree",)))
            mm = nc.tensor.matmul(psw[0:1, 0:16], onesc[:, 0:1],
                                  masksf[:, t * 16:(t + 1) * 16],
                                  start=True, stop=True)
            tcx.inc(mm, 1)
            nc.vector.wait_ge(tcx.h, tcx.n)
            vc.inc(nc.vector.tensor_copy(totrow[:, (t + 1) * 16:(t + 2) * 16],
                                         psw[0:1, 0:16]), 1)
            st[("pswfree2",)] = (vc, vc.n)
        for t in range(NT):
            if t == 0:
                vc.inc(nc.vector.tensor_copy(offrow[:, 0:16], totrow[:, 0:16]), 1)
            else:
                vc.inc(nc.vector.tensor_add(
                    offrow[:, t * 16:(t + 1) * 16],
                    offrow[:, (t - 1) * 16: t * 16],
                    totrow[:, t * 16:(t + 1) * 16]), 1)
            nc.vector.wait_ge(vc.h, vc.n)
        vc.inc(nc.vector.tensor_add(cntloc[:, :],
                                    offrow[:, (NT - 1) * 16: NT * 16],
                                    totrow[:, NT * 16:(NT + 1) * 16]), 1)
        st[("cnt",)] = (vc, vc.n)

        st[('p2a2_done',)] = True

    def emit_p2a2c():
        W(nc.gpsimd, ("cnt",))
        g_cci.dma(nc.gpsimd.dma_start(out=cci_d[:, :], in_=cntloc[:, :]))
        nc.gpsimd.wait_ge(g_cci.h, g_cci.n)
        ccs.inc(nc.gpsimd.collective_compute(
            "AllGather", OP.bypass, replica_groups=[list(range(NCORES))],
            ins=[cci_d.ap().opt()], outs=[cco_d.ap().opt()]), 1)
        st[('p2a2c_done',)] = True

    def emit_p2bh():
        if st.get(('p2a1_hi',), 0) < NT:
            emit_p2a1(st.get(('p2a1_hi',), 0), NT)
        if ('p2a1b_done',) not in st:
            emit_p2a1b()
        if ('p2a2_done',) not in st:
            emit_p2a2()
        if ('p2a2c_done',) not in st:
            emit_p2a2c()
        nc.sync.wait_ge(ccs.h, ccs.n)
        s_ccl.dma(nc.sync.dma_start(
            out=ccrow[0:1, :],
            in_=bass.AP(cco_d, 0, [[16 * NCORES, 1], [1, 16 * NCORES]])))

        nc.vector.wait_ge(s_ccl.h, s_ccl.n)
        vc.inc(nc.vector.tensor_mul(masked[:, :], ccrow[:, :], pmask[:, :]), 1)
        nc.vector.wait_ge(vc.h, vc.n)
        vc.inc(nc.vector.tensor_reduce(
            rowa[:, 0:16],
            bass.AP(masked, 0, [[16 * NCORES, 1], [1, 16], [16, NCORES]]),
            axis=AX, op=OP.add), 1)
        nc.vector.wait_ge(vc.h, vc.n)
        vc.inc(nc.vector.tensor_reduce(
            mvst[0:1, 0:E],
            bass.AP(ccrow, 0, [[16 * NCORES, 1], [1, E], [16, NCORES]]),
            axis=AX, op=OP.add), 1)
        nc.vector.wait_ge(vc.h, vc.n)
        vc.inc(nc.vector.tensor_add(rowa[:, E:16], rowa[:, E:16],
                                    mvst[0:1, 0:E]), 1)
        nc.vector.wait_ge(vc.h, vc.n)
        vc.inc(nc.vector.tensor_copy(
            bass.AP(rows48, 0, [[48 * NT, 1], [48, NT], [1, 16]]),
            bass.AP(offrow, 0, [[16 * NT, 1], [16, NT], [1, 16]])), 1)
        vc.inc(nc.vector.tensor_copy(totrow[:, 0:16], epad[0:1, :]), 1)
        nc.vector.wait_ge(vc.h, vc.n)
        vc.inc(nc.vector.tensor_add(totrow[:, 8:16], totrow[:, 8:16],
                                    cntloc[:, 0:E]), 1)
        nc.vector.wait_ge(vc.h, vc.n)
        for t in range(NT):
            vc.inc(nc.vector.tensor_copy(rows48[:, t * 48 + 16: t * 48 + 32],
                                         rowa[:, 0:16]), 1)
            vc.inc(nc.vector.tensor_copy(rows48[:, t * 48 + 32: t * 48 + 48],
                                         totrow[:, 0:16]), 1)
        nc.vector.wait_ge(vc.h, vc.n)
        st[('rows_ok',)] = vc.n
        st[('p2bh_done',)] = True

    def emit_p2b():
        if ('p2bh_done',) not in st:
            emit_p2bh()
        ROWS_OK = st[('rows_ok',)]

        NTH = NT // 4
        for hb in range(4):
            for tt in range(NTH):
                t = hb * NTH + tt
                nc.tensor.wait_ge(vc.h, ROWS_OK)
                W(nc.tensor, ("psrfree",))
                if hb:
                    W(nc.tensor, ("rkAfree",))
                nc.tensor.matmul(psr[:, 0:48], onesr[0:1, 0:128],
                                 rows48[:, t * 48:(t + 1) * 48], start=True, stop=False)
                mm = nc.tensor.matmul(psr[:, 0:16], tri[:, :],
                                      masksf[:, t * 16:(t + 1) * 16],
                                      start=False, stop=True)
                tcx.inc(mm, 1)
                nc.vector.wait_ge(tcx.h, tcx.n)
                vc.inc(nc.vector.tensor_copy(rkA[:, tt * 48:(tt + 1) * 48],
                                             psr[:, 0:48]), 1)
                st[("psrfree",)] = (vc, vc.n)
            nc.vector.wait_ge(vc.h, vc.n)
            v3 = lambda off: bass.AP(rkA, off, [[NTH * 48, 128], [48, NTH], [1, 16]])
            tAv = bass.AP(tA, 0, [[NTH * 16, 128], [16, NTH], [1, 16]])
            mfh = masksf[:, hb * NTH * 16:(hb + 1) * NTH * 16]
            vc.inc(nc.vector.tensor_tensor(tAv, v3(0), v3(16), op=OP.add), 1)
            nc.vector.wait_ge(vc.h, vc.n)
            vc.inc(nc.vector.tensor_single_scalar(tA[:, :], tA[:, :],
                                                  float(CAP), OP.is_lt), 1)
            nc.vector.wait_ge(vc.h, vc.n)
            vc.inc(nc.vector.tensor_mul(tA[:, :], tA[:, :], mfh), 1)
            nc.vector.wait_ge(vc.h, vc.n)
            vc.inc(nc.vector.tensor_reduce(
                gates[:, hb * NTH * 2:(hb + 1) * NTH * 2],
                bass.AP(tA, 0, [[NTH * 16, 128], [8, NTH * 2], [1, 8]]),
                axis=AX, op=OP.add), 1)
            nc.vector.wait_ge(vc.h, vc.n)
            vc.inc(nc.vector.tensor_mul(gates[:, hb * NTH * 2:(hb + 1) * NTH * 2],
                                        gates[:, hb * NTH * 2:(hb + 1) * NTH * 2],
                                        g0buf[:, hb * NTH * 2:(hb + 1) * NTH * 2]), 1)
            vc.inc(nc.vector.tensor_tensor(tAv, v3(0), v3(32), op=OP.add), 1)
            nc.vector.wait_ge(vc.h, vc.n)
            vc.inc(nc.vector.tensor_mul(tA[:, :], tA[:, :], mfh), 1)
            nc.vector.wait_ge(vc.h, vc.n)
            vc.inc(nc.vector.tensor_reduce(
                k2a[:, hb * NTH * 2:(hb + 1) * NTH * 2],
                bass.AP(tA, 0, [[NTH * 16, 128], [8, NTH * 2], [1, 8]]),
                axis=AX, op=OP.add), 1)
            nc.vector.wait_ge(vc.h, vc.n)
            st[("rkAfree",)] = (vc, vc.n)
            vc.inc(nc.vector.tensor_scalar_min(
                k2a[:, hb * NTH * 2:(hb + 1) * NTH * 2],
                k2a[:, hb * NTH * 2:(hb + 1) * NTH * 2], float(EPAD - 1)), 1)
            nc.vector.wait_ge(vc.h, vc.n)
            vc.inc(nc.vector.tensor_copy(slots[:, hb * NTH * 2:(hb + 1) * NTH * 2],
                                         k2a[:, hb * NTH * 2:(hb + 1) * NTH * 2]), 1)
            nc.vector.wait_ge(vc.h, vc.n)
        SLOTS_OK = vc.n
        for t in range(NT):
            st[("slot", t)] = (vc, SLOTS_OK)
        st[('P2_VC',)] = vc.n

        st[('p2_done',)] = True

    def emit_scatters():
        for t in range(NT):
            xs = 0
            W(nc.gpsimd, ("xns_scat", xs))
            W(nc.gpsimd, ("xnsp", t))
            g_xsl.dma(nc.gpsimd.dma_start(out=xns[xs][:, :],
                                          in_=xnsp_d[t * 128:(t + 1) * 128, :]))
            st[("xnrl", t)] = (g_xsl, g_xsl.n)
            W(nc.gpsimd, ("xnrl", t))
            W(nc.gpsimd, ("slot", t))
            if t == 0:
                nc.gpsimd.wait_ge(s_pre.h, s_pre.n)
            for j in range(2):
                g_sct[xs].dma(nc.gpsimd.indirect_dma_start(
                    out=disp_d[:, :],
                    out_offset=bass.IndirectOffsetOnAxis(
                        ap=slots[:, t * 2 + j: t * 2 + j + 1], axis=0),
                    in_=xns[xs][:, :], in_offset=None))
            st[("xns_scat", xs)] = (g_sct[xs], g_sct[xs].n)


    NGTS = 4 * len(blocks)
    fslot = 0

    def emit_ln(gt):
        src_h, roff, own = blocks[gt // 4]
        i = gt % 4
        xs = gt % 2
        W(nc.gpsimd, ("xtile_free", xs))
        g_xld[xs].dma(nc.gpsimd.dma_start(
            out=xtile[xs][:, :],
            in_=src_h[roff + i * 128: roff + (i + 1) * 128, :]))
        st[("xload", gt)] = (g_xld[xs], g_xld[xs].n)

        W(nc.vector, ("xload", gt))
        nst = (D + 511) // 512
        for c_ in range(nst):
            w_ = min(512, D - c_ * 512)
            vc.inc(nc.vector.bn_stats(
                mvst[:, 6 * c_:6 * c_ + 6],
                xtile[xs][:, c_ * 512: c_ * 512 + w_]), 1)
            nc.vector.wait_ge(vc.h, vc.n)
        vc.inc(nc.vector.bn_aggr(mvst[:, 12:14], mvst[:, 0:6 * nst]), 1)
        nc.vector.wait_ge(vc.h, vc.n)
        vc.inc(nc.vector.tensor_scalar_add(mvst[:, 14:15], mvst[:, 13:14], 1e-5), 1)
        nc.scalar.wait_ge(vc.h, vc.n)
        acx.inc(nc.scalar.activation(mvst[:, 15:16], mvst[:, 14:15], AF.Sqrt), 1)
        nc.vector.wait_ge(acx.h, acx.n)
        vc.inc(nc.vector.reciprocal(mvst[:, 15:16], mvst[:, 15:16]), 1)
        nc.vector.wait_ge(vc.h, vc.n)
        xslot = gt % 2
        W(nc.vector, ("xnt_free", xslot))
        W(nc.vector, ("xnt_spill", xslot))
        vc.inc(nc.vector.tensor_scalar(
            xnt[xslot][:, :], xtile[xs][:, :], mvst[:, 12:13], mvst[:, 15:16],
            op0=OP.subtract, op1=OP.mult), 1)
        st[("xn", gt)] = (vc, vc.n)
        st[("xtile_free", xs)] = (vc, vc.n)

        if own:
            W(nc.sync, ("xn", gt))
            s_xnw[xslot].dma(nc.sync.dma_start(
                out=xnsp_d[gt * 128:(gt + 1) * 128, :], in_=xnt[xslot][:, :]))
            st[("xnt_spill", xslot)] = (s_xnw[xslot], s_xnw[xslot].n)
            st[("xnsp", gt)] = (s_xnw[xslot], s_xnw[xslot].n)

    for tb, (src_h, roff, own) in enumerate(blocks):
        if tb >= 1:
            emit_prefill_chunk(tb - 1, max(1, len(blocks) - 1))
        if cfg.get("p2a_early", True) and tb == NOB + 2:
            if st.get(('p2a1_hi',), 0) < NT:
                emit_p2a1(st.get(('p2a1_hi',), 0), NT)
            emit_p2a1b()
            emit_p2a2()
        for i in range(4):
            gt = tb * 4 + i
            xslot = gt % 2
            if gt == 0:
                emit_ln(0)
                emit_ln(1)
                W(nc.tensor, ("ident",))

            W(nc.tensor, ("xn", gt))
            pstile = [pst0, pst1][gt % 2]
            W(nc.tensor, ("trev", gt % 2))
            for dcp in range(DC):
                tcx.inc(nc.tensor.transpose(
                    pstile[:, dcp * 128:(dcp + 1) * 128],
                    xnt[xslot][:, dcp * 128:(dcp + 1) * 128],
                    ident[:, :]), 1)
            st[("tr", gt)] = (tcx, tcx.n)
            W(nc.vector, ("tr", gt))
            vc.inc(nc.vector.tensor_copy(
                bass.AP(xnf[fslot], i * 128, [[DC * 512, 128], [512, DC], [1, 128]]),
                pstile[:, 0:DC * 128].rearrange("p (c k) -> p c k", c=DC)), 1)
            st[("trev", gt % 2)] = (vc, vc.n)
            st[("xnt_free", xslot)] = (tcx, tcx.n)
            st[("xnf", tb, i)] = (vc, vc.n)
            if gt + 2 < NGTS:
                emit_ln(gt + 2)
            if gt < 4:
                emit_w3_piece(2 * gt)
                emit_w3_piece(2 * gt + 1)

            if cfg.get("p2a_early", True):
                q4 = NT // 4 if NT >= 4 else 0
                if q4 and tb == NOB + 1:
                    emit_p2a1(i * q4, (i + 1) * q4)
                if tb == NOB + 2 and i == 0 and ('p2a2_done',) in st:
                    emit_p2a2c()

        for i in range(4):
            W(nc.tensor, ("xnf", tb, i))
        pi = 0
        wlist = [(1, "k")] + ([(0, "q")] if own else [])
        for wsel, nm in wlist:
            W(nc.tensor, ("w3k",) if nm == "k" else ("w3q",))
            for oc in range(DC):
                p = pspool[pi % 4]; pi += 1
                W(nc.tensor, ("psfree", p.name))
                for hc in range(DC):
                    mm = nc.tensor.matmul(
                        p[:, 0:512],
                        w3t[:, wsel * DC * D + hc * D + oc * 128:
                            wsel * DC * D + hc * D + oc * 128 + 128],
                        xnf[fslot][:, hc * 512:(hc + 1) * 512],
                        start=(hc == 0), stop=(hc == DC - 1))
                tcx.inc(mm, 1)
                nc.vector.wait_ge(tcx.h, tcx.n)
                if nm == "k":
                    if oc == DC - 1 and tb == NBF - 1:
                        nc.vector.wait_ge(s_pre.h, s_pre.n)
                    vc.inc(nc.vector.tensor_copy(
                        kf[:, oc * SF + tb * 512: oc * SF + (tb + 1) * 512],
                        p[:, 0:512]), 1)
                else:
                    qb_ = oc % 2
                    qt_ = qevb[0] if qb_ == 0 else qevb1
                    W(nc.vector, ("qevb_used", qb_))
                    vc.inc(nc.vector.tensor_copy(qt_[:, :], p[:, 0:512]), 1)
                    nc.sync.wait_ge(vc.h, vc.n)
                    s_qst[qb_].dma(nc.sync.dma_start(
                        out=qsp_d[oc, :, tb * 512:(tb + 1) * 512],
                        in_=qt_[:, :]))
                    st[("qevb_used", qb_)] = (s_qst[qb_], s_qst[qb_].n)
                st[("psfree", p.name)] = (vc, vc.n)
        W(nc.tensor, ("w3v",))
        for tc4 in range(4):
            for dh in range(DHN):
                p = pspool[pi % 4]; pi += 1
                W(nc.tensor, ("psfree", p.name))
                for hc in range(DC):
                    mm = nc.tensor.matmul(
                        p[:, 0:DW],
                        xnf[fslot][:, hc * 512 + tc4 * 128: hc * 512 + (tc4 + 1) * 128],
                        w3t[:, 2 * DC * D + hc * D + dh * DW:
                            2 * DC * D + hc * D + (dh + 1) * DW],
                        start=(hc == 0), stop=(hc == DC - 1))
                tcx.inc(mm, 1)
                kc = tb * 4 + tc4
                nc.vector.wait_ge(tcx.h, tcx.n)
                vc.inc(nc.vector.tensor_copy(
                    vt[:, kc * D + dh * DW: kc * D + (dh + 1) * DW],
                    p[:, 0:DW]), 1)
                st[("psfree", p.name)] = (vc, vc.n)
        if own:
            W(nc.tensor, ("wg",))
            W(nc.tensor, ("pswfree",))
            for hc in range(DC):
                mm = nc.tensor.matmul(
                    psw[0:E, 0:512], wgsb[:, hc * E:(hc + 1) * E],
                    xnf[fslot][:, hc * 512:(hc + 1) * 512],
                    start=(hc == 0), stop=(hc == DC - 1))
            tcx.inc(mm, 1)
            nc.vector.wait_ge(tcx.h, tcx.n)
            vc.inc(nc.vector.tensor_copy(logp[0:E, :], psw[0:E, 0:512]), 1)
            st[("pswfree",)] = (vc, vc.n)
            st[("logev", tb)] = (vc, vc.n)
            W(nc.tensor, ("logev", tb))
            for i in range(4):
                gt = tb * 4 + i
                pstile = [pst0, pst1][i % 2]
                W(nc.tensor, ("trev", i % 2))
                tcx.inc(nc.tensor.transpose(
                    pstile[0:128, 0:E], logp[0:E, i * 128:(i + 1) * 128],
                    ident[0:E, 0:E]), 1)
                st[("ltr", gt)] = (tcx, tcx.n)
                W(nc.vector, ("ltr", gt))
                vc.inc(nc.vector.tensor_copy(
                    logt[:, gt * E:(gt + 1) * E], pstile[0:128, 0:E]), 1)
                st[("trev", i % 2)] = (vc, vc.n)
                st[("logt", gt)] = (vc, vc.n)
        st[("xnf_free", fslot)] = (tcx, tcx.n)

    emit_p2b()
    P1_TC = tcx.n
    P1_VC = vc.n
    stkA.close()

    wot = SBB("wot", [128, DC * D], BF16)
    qtile = [SBB(f"qtile{i}", [128, DC * 128], BF16) for i in range(2)]
    scoresb = SBB("scoresb", [128, SF], F32)
    probs = [SBB(f"probsb{i}", [128, SF], BF16) for i in range(2)]
    probsT = SBB("probsT", [128, KC * QH], BF16)
    ctxf = SBB("ctxf", [128, DC * QH], BF16)
    aot = [SBB("aot0", [128, 512], BF16)]
    NKC = SF // 512
    cmax = SBB("cmax", [128, 2 * NKC], F32)

    nc.gpsimd.wait_ge(vc.h, st[('P2_VC',)])
    s_wol.dma(nc.gpsimd.dma_start(
        out=wot[:, :].rearrange("p (c o) -> p c o", c=DC),
        in_=bass.AP(wot_h, 0, [[D, 128], [128 * D, DC], [1, D]])))
    emit_scatters()
    SCAT = [(g_sct[0], g_sct[0].n), (g_sct[1], g_sct[1].n)]

    nc.tensor.wait_ge(vc.h, P1_VC)

    def emit_scores(gt, qs):
        gp_ = gt % 2
        W(nc.tensor, ("qldt", gt))
        for kc8 in range(SF // 512):
            p = pspool[kc8 % 2]
            W(nc.tensor, ("psfree", p.name))
            for hc in range(DC):
                mm = nc.tensor.matmul(
                    p[:, 0:512],
                    qtile[qs][:, hc * 128:(hc + 1) * 128],
                    kf[:, hc * SF + kc8 * 512: hc * SF + (kc8 + 1) * 512],
                    start=(hc == 0), stop=(hc == DC - 1))
            tcx.inc(mm, 1)
            nc.vector.wait_ge(tcx.h, tcx.n)
            if kc8 == 0:
                W(nc.vector, ("exp_prev",))
            vc.inc(nc.vector.tensor_copy(
                scoresb[:, kc8 * 512:(kc8 + 1) * 512], p[:, 0:512]), 1)
            vc.inc(nc.vector.tensor_reduce(
                cmax[:, gp_ * NKC + kc8: gp_ * NKC + kc8 + 1],
                p[:, 0:512], axis=AX, op=OP.max), 1)
            st[("psfree", p.name)] = (vc, vc.n)
        st[("scev", gt)] = (vc, vc.n)
        st[("qtile_used", gt % 2)] = (tcx, tcx.n)

    def emit_softmax(gt):
        gp_ = gt % 2
        W(nc.vector, ("scev", gt))
        W(nc.vector, ("exp_prev2", gp_))
        vc.inc(nc.vector.reduce_max(mvst[:, 4 * gp_:4 * gp_ + 1],
                                    cmax[:, gp_ * NKC: gp_ * NKC + NKC], axis=AX), 1)
        nc.vector.wait_ge(vc.h, vc.n)
        vc.inc(nc.vector.tensor_scalar_mul(
            mvst[:, 4 * gp_ + 1:4 * gp_ + 2],
            mvst[:, 4 * gp_:4 * gp_ + 1], -SCALE), 1)
        nc.scalar.wait_ge(vc.h, vc.n)
        W(nc.scalar, ("ptr_last", gp_))
        acx.inc(nc.scalar.activation(
            probs[gp_][:, :], scoresb[:, :], AF.Exp,
            bias=mvst[:, 4 * gp_ + 1:4 * gp_ + 2],
            scale=SCALE, accum_out=mvst[:, 4 * gp_ + 2:4 * gp_ + 3]), 1)
        st[("exp", gt)] = (acx, acx.n)
        st[("exp_prev",)] = (acx, acx.n)
        st[("exp_prev2", gp_)] = (acx, acx.n)
        W(nc.vector, ("exp", gt))
        vc.inc(nc.vector.reciprocal(sums[:, gt:gt + 1],
                                    mvst[:, 4 * gp_ + 2:4 * gp_ + 3]), 1)

    def emit_transposes(gt):
        gp_ = gt % 2
        ti = gt % 2
        W(nc.tensor, ("exp", gt))
        for kb in range(0, KC, 8):
            nb = min(8, KC - kb)
            pstile = [pst0, pst1][(kb // 8) % 2]
            W(nc.tensor, ("trev", (kb // 8) % 2))
            if kb == 0:
                W(nc.tensor, ("probsT_done",))
            for k2_ in range(nb):
                kc = kb + k2_
                tcx.inc(nc.tensor.transpose(
                    pstile[:, k2_ * 128:(k2_ + 1) * 128],
                    probs[gp_][:, kc * 128:(kc + 1) * 128],
                    ident[:, :]), 1)
            st[("ptr", gt, kb)] = (tcx, tcx.n)
            W(nc.vector, ("ptr", gt, kb))
            vc.inc(nc.vector.tensor_copy(
                bass.AP(probsT, kb * QH + ti * 128,
                        [[KC * QH, 128], [QH, nb], [1, 128]]),
                pstile[:, 0:nb * 128].rearrange("p (k c) -> p k c", k=nb)), 1)
            st[("trev", (kb // 8) % 2)] = (vc, vc.n)
        st[("ptr_done", gt)] = (vc, vc.n)
        st[("ptr_last", gp_)] = st[("ptr", gt, KC - 8 if KC >= 8 else 0)]

    def emit_ctx_ao(h):
        W(nc.tensor, ("ptr_done", 2 * h))
        W(nc.tensor, ("ptr_done", 2 * h + 1))
        for hc in range(DC):
            p = pspool[2 + (hc % 2)]
            W(nc.tensor, ("psfree", p.name))
            for kc in range(KC):
                mm = nc.tensor.matmul(
                    p[:, 0:QH],
                    vt[:, kc * D + hc * 128: kc * D + (hc + 1) * 128],
                    probsT[:, kc * QH:(kc + 1) * QH],
                    start=(kc == 0), stop=(kc == KC - 1))
            tcx.inc(mm, 1)
            nc.vector.wait_ge(tcx.h, tcx.n)
            vc.inc(nc.vector.tensor_copy(
                ctxf[:, hc * QH:(hc + 1) * QH], p[:, 0:QH]), 1)
            st[("psfree", p.name)] = (vc, vc.n)
        st[("ctx_done",)] = (vc, vc.n)
        st[("probsT_done",)] = (tcx, tcx.n)
        nc.tensor.wait_ge(s_wol.h, s_wol.n)
        W(nc.tensor, ("ctx_done",))
        for qc2 in range(2):
            gt = 2 * h + qc2
            asl = 0
            for dh in range(DHN):
                p = pspool[dh % 2]
                W(nc.tensor, ("psfree", p.name))
                for hc in range(DC):
                    mm = nc.tensor.matmul(
                        p[:, 0:DW],
                        ctxf[:, hc * QH + qc2 * 128: hc * QH + (qc2 + 1) * 128],
                        wot[:, hc * D + dh * DW: hc * D + (dh + 1) * DW],
                        start=(hc == 0), stop=(hc == DC - 1))
                tcx.inc(mm, 1)
                nc.vector.wait_ge(tcx.h, tcx.n)
                W(nc.vector, ("aot_used", asl))
                vc.inc(nc.vector.tensor_scalar_mul(
                    aot[asl][:, 0:DW], p[:, 0:DW],
                    sums[:, gt:gt + 1]), 1)
                st[("psfree", p.name)] = (vc, vc.n)
                nc.sync.wait_ge(vc.h, vc.n)
                s_aos[asl].dma(nc.sync.dma_start(
                    out=aosp_d[gt * 128:(gt + 1) * 128, dh * DW:(dh + 1) * DW],
                    in_=aot[asl][:, 0:DW]))
                st[("aot_used", asl)] = (s_aos[asl], s_aos[asl].n)

    def emit_qld(gt):
        qs = gt % 2
        W(nc.sync, ("qtile_used", qs))
        if gt < 2:
            nc.sync.wait_ge(s_qst[0].h, s_qst[0].n)
            nc.sync.wait_ge(s_qst[1].h, s_qst[1].n)
            nc.sync.wait_ge(vc.h, st[('P2_VC',)])
            nc.sync.wait_ge(tcx.h, P1_TC)
        s_qsl[qs].dma(nc.sync.dma_start(
            out=qtile[qs][:, :].rearrange("p (c q) -> p c q", c=DC),
            in_=bass.AP(qsp_d, gt * 128, [[TOK, 128], [128 * TOK, DC], [1, 128]])))
        st[("qldt", gt)] = (s_qsl[qs], s_qsl[qs].n)

    NGT = 4 * NQB
    prev = None
    for gt in range(NGT):
        if gt == 0:
            emit_qld(0)
            emit_qld(1)
        emit_scores(gt, gt % 2)
        if gt + 2 < NGT:
            emit_qld(gt + 2)
        if prev is not None:
            emit_transposes(prev)
        emit_softmax(gt)
        if prev is not None and prev % 2 == 1:
            emit_ctx_ao(prev // 2)
        prev = gt
    emit_transposes(prev)
    emit_ctx_ao(prev // 2)
    P3_TC = tcx.n
    P3_VC = vc.n
    P3_AC = acx.n
    stkB.close()

    gidx = SBC("gidx_sb", [128, E * GIW], I16)
    dispb = [SBC(f"dispb{i}", [128, DC * PAD], BF16) for i in range(2)]
    f1sl = [SBC(f"f1sl{i}", [128, DC * HS], BF16) for i in range(2)]
    f2sb = SBC("f2sb", [128, HC * D], BF16)
    hid = SBC("hid", [128, HC * PAD], BF16)
    f1bs = SBC("f1bs", [128, HC], F32)
    f2br = SBC("f2br", [1, D], F32)
    b2bc = SBC("b2bc", [128, D], F32)
    onesr4 = SBC("onesr4", [1, 128], F32)
    ytile = [SBC(f"ytile{i}", [128, D], BF16) for i in range(2)]

    for eng in [nc.gpsimd, nc.sync]:
        eng.wait_ge(tcx.h, P3_TC)
        eng.wait_ge(vc.h, P3_VC)
        eng.wait_ge(acx.h, P3_AC)
        eng.wait_ge(s_aos[0].h, s_aos[0].n)
        eng.wait_ge(s_aos[1].h, s_aos[1].n)
        eng.wait_ge(SCAT[0][0].h, SCAT[0][1])
        eng.wait_ge(SCAT[1][0].h, SCAT[1][1])
    s_cst.dma(nc.sync.dma_start(out=gidx[:, :], in_=gidx_h[:, :]))
    SQ_CONST = s_cst.n
    nc.gpsimd.wait_ge(s_cst.h, SQ_CONST)
    nc.gpsimd.wait_ge(SCAT[0][0].h, SCAT[0][1])
    nc.gpsimd.wait_ge(SCAT[1][0].h, SCAT[1][1])
    nc.vector.wait_ge(tcx.h, P3_TC)
    vc.inc(nc.vector.memset(onesr4[:, :], 1.0), 1)
    ONESR4_OK = vc.n
    nc.tensor.wait_ge(vc.h, ONESR4_OK)
    for e in range(E):
        ds = e % 2
        W(nc.gpsimd, ("disp_used", ds))
        g_gth[ds].dma(nc.gpsimd.dma_gather(
            dispb[ds][:, :].rearrange("p (c s) -> p c s", c=DC),
            disp_d[:, :], gidx[:, e * GIW:(e + 1) * GIW], PAD, PAD, D,
            transpose=True))
        st[("disp", e)] = (g_gth[ds], g_gth[ds].n)
        W(nc.sync, ("f1b_used",))
        W(nc.sync, ("f2b_used",))
        s_bld.dma(nc.sync.dma_start(
            out=f1bs[:, :], in_=bass.AP(fc1b_h, e * H, [[1, 128], [128, HC]]),
            allow_slow_non_contiguous=True))
        s_bld.dma(nc.sync.dma_start(out=f2br[:, :], in_=fc2b_h[e:e + 1, :]))
        st[("bld", e)] = (s_bld, s_bld.n)
        for hs in range(NHS):
            fs = hs % 2
            W(nc.gpsimd, ("f1sl_used", fs))
            g_f1l[fs].dma(nc.gpsimd.dma_start(
                out=f1sl[fs][:, :],
                in_=fc1s_h[(e * NHS + hs) * 128:(e * NHS + hs + 1) * 128, :]))
            st[("f1sl", e, hs)] = (g_f1l[fs], g_f1l[fs].n)
            HQ = HC // 4
            for q_ in range(4):
                if hs == min(1 + q_, NHS - 1) and ("f2q_em", e, q_) not in st:
                    st[("f2q_em", e, q_)] = True
                    qsem = g_f2q[q_]
                    W(nc.gpsimd, ("f2_used",))
                    qsem.dma(nc.gpsimd.dma_start(
                        out=f2sb[:, q_ * HQ * D:(q_ + 1) * HQ * D],
                        in_=fc2r_h[e * 128:(e + 1) * 128,
                                   q_ * HQ * D:(q_ + 1) * HQ * D]))
                    st[("f2q", e, q_)] = (qsem, qsem.n)
            W(nc.tensor, ("f1sl", e, hs))
            if hs == 0:
                W(nc.tensor, ("disp", e))
            for hcc in range(HSC):
                hhc = hs * HSC + hcc
                for sbk in range(NSB):
                    p = pspool[(hcc * NSB + sbk) % 4]
                    W(nc.tensor, ("psfree", p.name))
                    for dc in range(DC):
                        mm = nc.tensor.matmul(
                            p[:, 0:SBW],
                            f1sl[fs][:, dc * HS + hcc * 128: dc * HS + (hcc + 1) * 128],
                            dispb[ds][:, dc * PAD + sbk * SBW:
                                      dc * PAD + (sbk + 1) * SBW],
                            start=(dc == 0), stop=(dc == DC - 1))
                    tcx.inc(mm, 1)
                    nc.scalar.wait_ge(tcx.h, tcx.n)
                    W(nc.scalar, ("bld", e))
                    acx.inc(nc.scalar.activation(
                        hid[:, hhc * PAD + sbk * SBW: hhc * PAD + (sbk + 1) * SBW],
                        p[:, 0:SBW], AF.Silu, bias=f1bs[:, hhc:hhc + 1]), 1)
                    st[("psfree", p.name)] = (acx, acx.n)
            st[("f1sl_used", fs)] = (acx, acx.n)
        st[("disp_used", ds)] = (tcx, tcx.n)
        st[("f1b_used",)] = (acx, acx.n)
        HID_OK = acx.n


        nc.tensor.wait_ge(acx.h, HID_OK)
        W(nc.tensor, ("bld", e))
        p = pspool[2]
        for dh in range(DHN):
            W(nc.tensor, ("psfree", p.name))
            mm = nc.tensor.matmul(p[:, 0:DW], onesr4[0:1, 0:128],
                                  f2br[:, dh * DW:(dh + 1) * DW],
                                  start=True, stop=True)
            tcx.inc(mm, 1)
            nc.vector.wait_ge(tcx.h, tcx.n)
            W(nc.vector, ("b2bc_used",))
            vc.inc(nc.vector.tensor_copy(b2bc[:, dh * DW:(dh + 1) * DW],
                                         p[:, 0:DW]), 1)
            st[("psfree", p.name)] = (vc, vc.n)
        st[("b2bc",)] = (vc, vc.n)
        W(nc.vector, ("b2bc",))
        for sc in range(NSL):
            ys = sc % 2
            for dh in range(DHN):
                p = pspool[dh % 2]
                W(nc.tensor, ("psfree", p.name))
                for hhc in range(HC):
                    if sc == 0 and dh == 0 and hhc % (HC // 4) == 0:
                        W(nc.tensor, ("f2q", e, hhc // (HC // 4)))
                    mm = nc.tensor.matmul(
                        p[:, 0:DW],
                        hid[:, hhc * PAD + sc * 128: hhc * PAD + (sc + 1) * 128],
                        f2sb[:, hhc * D + dh * DW: hhc * D + (dh + 1) * DW],
                        start=(hhc == 0), stop=(hhc == HC - 1))
                tcx.inc(mm, 1)
                nc.vector.wait_ge(tcx.h, tcx.n)
                W(nc.vector, ("ytile_used", ys))
                vc.inc(nc.vector.tensor_tensor(
                    ytile[ys][:, dh * DW:(dh + 1) * DW], p[:, 0:DW],
                    b2bc[:, dh * DW:(dh + 1) * DW], op=OP.add), 1)
                st[("psfree", p.name)] = (vc, vc.n)
            st[("yev", e, sc)] = (vc, vc.n)
            W(nc.sync, ("yev", e, sc))
            s_yst[ys].dma(nc.sync.dma_start(
                out=ybuf_d[e * PAD + sc * 128: e * PAD + (sc + 1) * 128, :],
                in_=ytile[ys][:, :]))
            st[("ytile_used", ys)] = (s_yst[ys], s_yst[ys].n)
        st[("f2_used",)] = (tcx, tcx.n)
        st[("f2b_used",)] = (tcx, tcx.n)
        st[("b2bc_used",)] = (tcx, tcx.n)
    YST = [(s_yst[0], s_yst[0].n), (s_yst[1], s_yst[1].n)]
    P4_TC = tcx.n
    P4_VC = vc.n
    P4_AC = acx.n
    stkC.close()

    lngb = SB("lngb", [128, D], BF16)
    lnbb = SB("lnbb", [128, D], BF16)
    yg = [SB(f"yg{i}", [128, 2 * D], BF16) for i in range(2)]
    xno = [SB(f"xno{i}", [128, D], BF16) for i in range(2)]
    xrl = [SB(f"xrl{i}", [128, D], F32) for i in range(2)]
    aorl = [SB(f"aorl{i}", [128, D], BF16) for i in range(2)]
    rest_ = [SB(f"rest{i}", [128, D], F32) for i in range(2)]
    outt = [SB(f"outt{i}", [128, D], F32) for i in range(2)]

    for eng in [nc.gpsimd, nc.sync]:
        eng.wait_ge(YST[0][0].h, YST[0][1])
        eng.wait_ge(YST[1][0].h, YST[1][1])
        eng.wait_ge(tcx.h, P4_TC)
        eng.wait_ge(vc.h, P4_VC)
        eng.wait_ge(acx.h, P4_AC)
        for q_ in range(4):
            eng.wait_ge(g_f2q[q_].h, g_f2q[q_].n)
        eng.wait_ge(g_gth[0].h, g_gth[0].n)
        eng.wait_ge(g_gth[1].h, g_gth[1].n)
        eng.wait_ge(g_f1l[0].h, g_f1l[0].n)
        eng.wait_ge(g_f1l[1].h, g_f1l[1].n)
        eng.wait_ge(s_bld.h, s_bld.n)
    g_cst.dma(nc.gpsimd.dma_start(
        out=lngb[:, :], in_=bass.AP(lng_h, 0, [[0, 128], [1, D]])))
    g_cst.dma(nc.gpsimd.dma_start(
        out=lnbb[:, :], in_=bass.AP(lnb_h, 0, [[0, 128], [1, D]])))
    nc.sync.wait_ge(s_aos[0].h, s_aos[0].n)
    nc.sync.wait_ge(s_aos[1].h, s_aos[1].n)
    nc.sync.wait_ge(s_xnw[0].h, s_xnw[0].n)
    nc.sync.wait_ge(s_xnw[1].h, s_xnw[1].n)
    for t in range(NT):
        ts = t % 2
        W(nc.gpsimd, ("yg_used", ts))
        for j in range(2):
            g_ygl[ts].dma(nc.gpsimd.indirect_dma_start(
                out=yg[ts][:, j * D:(j + 1) * D], out_offset=None,
                in_=ybuf_d[:, :],
                in_offset=bass.IndirectOffsetOnAxis(
                    ap=slots[:, t * 2 + j: t * 2 + j + 1], axis=0)))
        st[("yg", t)] = (g_ygl[ts], g_ygl[ts].n)
        W(nc.sync, ("xno_used", ts))
        s_x5a[ts].dma(nc.sync.dma_start(out=xno[ts][:, :],
                                        in_=xnsp_d[t * 128:(t + 1) * 128, :]))
        st[("xno", t)] = (s_x5a[ts], s_x5a[ts].n)
        W(nc.sync, ("xrl_used", ts))
        s_xrl[ts].dma(nc.sync.dma_start(out=xrl[ts][:, :],
                                        in_=xo_h[t * 128:(t + 1) * 128, :]))
        st[("xrl", t)] = (s_xrl[ts], s_xrl[ts].n)
        W(nc.sync, ("aorl_used", ts))
        s_x5c[ts].dma(nc.sync.dma_start(out=aorl[ts][:, :],
                                        in_=aosp_d[t * 128:(t + 1) * 128, :]))
        st[("aorl", t)] = (s_x5c[ts], s_x5c[ts].n)

        W(nc.vector, ("yg", t))
        W(nc.vector, ("xno", t))
        W(nc.vector, ("xrl", t))
        W(nc.vector, ("aorl", t))
        vc.inc(nc.vector.tensor_scalar_mul(rest_[ts][:, :], yg[ts][:, 0:D],
                                           gates[:, t * 2: t * 2 + 1]), 1)
        nc.vector.wait_ge(vc.h, vc.n)
        vc.inc(nc.vector.scalar_tensor_tensor(
            rest_[ts][:, :], yg[ts][:, D:2 * D], gates[:, t * 2 + 1: t * 2 + 2],
            rest_[ts][:, :], op0=OP.mult, op1=OP.add), 1)
        nc.vector.wait_ge(vc.h, vc.n)
        st[("yg_used", ts)] = (vc, vc.n)
        vc.inc(nc.vector.tensor_add(rest_[ts][:, :], rest_[ts][:, :], xno[ts][:, :]), 1)
        st[("xno_used", ts)] = (vc, vc.n)
        nc.vector.wait_ge(vc.h, vc.n)
        nst = (D + 511) // 512
        for c_ in range(nst):
            w_ = min(512, D - c_ * 512)
            vc.inc(nc.vector.bn_stats(mvst[:, 6 * c_:6 * c_ + 6],
                                      rest_[ts][:, c_ * 512: c_ * 512 + w_]), 1)
            nc.vector.wait_ge(vc.h, vc.n)
        vc.inc(nc.vector.bn_aggr(mvst[:, 12:14], mvst[:, 0:6 * nst]), 1)
        nc.vector.wait_ge(vc.h, vc.n)
        vc.inc(nc.vector.tensor_scalar_add(mvst[:, 14:15], mvst[:, 13:14], 1e-5), 1)
        nc.scalar.wait_ge(vc.h, vc.n)
        acx.inc(nc.scalar.activation(mvst[:, 15:16], mvst[:, 14:15], AF.Sqrt), 1)
        nc.vector.wait_ge(acx.h, acx.n)
        vc.inc(nc.vector.reciprocal(mvst[:, 15:16], mvst[:, 15:16]), 1)
        nc.vector.wait_ge(vc.h, vc.n)
        os_ = t % 2
        W(nc.vector, ("outt_used", os_))
        vc.inc(nc.vector.tensor_scalar(outt[os_][:, :], rest_[ts][:, :], mvst[:, 12:13],
                                       mvst[:, 15:16], op0=OP.subtract, op1=OP.mult), 1)
        nc.vector.wait_ge(vc.h, vc.n)
        nc.vector.wait_ge(g_cst.h, g_cst.n)
        vc.inc(nc.vector.tensor_mul(outt[os_][:, :], outt[os_][:, :], lngb[:, :]), 1)
        nc.vector.wait_ge(vc.h, vc.n)
        vc.inc(nc.vector.tensor_add(outt[os_][:, :], outt[os_][:, :], lnbb[:, :]), 1)
        nc.vector.wait_ge(vc.h, vc.n)
        vc.inc(nc.vector.tensor_add(outt[os_][:, :], outt[os_][:, :],
                                    xrl[ts][:, :]), 1)
        st[("xrl_used", ts)] = (vc, vc.n)
        nc.vector.wait_ge(vc.h, vc.n)
        vc.inc(nc.vector.tensor_add(outt[os_][:, :], outt[os_][:, :],
                                    aorl[ts][:, :]), 1)
        st[("aorl_used", ts)] = (vc, vc.n)
        st[("outv", t)] = (vc, vc.n)
        W(nc.sync, ("outv", t))
        s_ost[os_].dma(nc.sync.dma_start(out=out_h[t * 128:(t + 1) * 128, :],
                                         in_=outt[os_][:, :]))
        st[("outt_used", os_)] = (s_ost[os_], s_ost[os_].n)

    nc.sync.wait_ge(s_ost[0].h, s_ost[0].n)
    nc.sync.wait_ge(s_ost[1].h, s_ost[1].n)
    stk.close()
    return nc


def _host_prep(cfg, x_own, x_oth, W_qkv, W_out, wg, fc1_w, fc1_b, fc2_w, fc2_b,
               ln_g, ln_b, core_id):
    D = cfg["D"]; E = cfg["E"]; PAD = cfg["PAD"]; NC_ = cfg["NCORES"]
    GIW = PAD // 16
    f32 = np.float32
    Wq = np.ascontiguousarray(W_qkv[0::3, :].T, dtype=f32)
    Wk = np.ascontiguousarray(W_qkv[1::3, :].T, dtype=f32)
    Wv = np.ascontiguousarray(W_qkv[2::3, :].T, dtype=f32)
    Wo = np.ascontiguousarray(W_out.T, dtype=f32)
    E_, H_, D_ = fc1_w.shape
    DCh, HCh = D_ // 128, H_ // 128
    HSh = min(H_, 1024); NHSh = H_ // HSh
    fc1s = np.ascontiguousarray(
        fc1_w.transpose(0, 2, 1).reshape(E_, DCh, 128, NHSh, HSh)
        .transpose(0, 3, 2, 1, 4).reshape(E_ * NHSh * 128, DCh * HSh),
        dtype=f32)
    fc2r = np.ascontiguousarray(
        fc2_w.reshape(E_, HCh, 128, D_).transpose(0, 2, 1, 3)
        .reshape(E_ * 128, HCh * D_), dtype=f32)
    tri = (np.arange(128)[None, :] > np.arange(128)[:, None]).astype(f32)
    ident = np.eye(128, dtype=f32)
    e8 = np.broadcast_to(np.arange(E, dtype=f32)[None, :], (128, E)).copy()
    epad = np.broadcast_to(
        np.array([e * PAD for j in range(2) for e in range(E)], f32)[None, :],
        (128, 2 * E)).copy()
    pmask = np.zeros((1, 16 * NC_), f32)
    pmask[0, :16 * core_id] = 1.0
    gi = (np.arange(E * GIW, dtype=np.int64)[None, :] * 16
          + (np.arange(128) % 16)[:, None]).astype(np.int16)
    return {
        "x_own": np.ascontiguousarray(x_own, dtype=f32),
        "x_oth": np.ascontiguousarray(x_oth, dtype=f32),
        "wq_t": Wq, "wk_t": Wk, "wv_t": Wv, "wo_t": Wo,
        "wg": np.ascontiguousarray(wg, dtype=f32),
        "fc1s": fc1s, "fc2r": fc2r,
        "fc1b": np.ascontiguousarray(fc1_b, dtype=f32),
        "fc2b": np.ascontiguousarray(fc2_b, dtype=f32),
        "ln_g": np.ascontiguousarray(ln_g.reshape(1, D), dtype=f32),
        "ln_b": np.ascontiguousarray(ln_b.reshape(1, D), dtype=f32),
        "tri": tri, "ident": ident, "e8": e8, "epad_bc": epad,
        "pmask": pmask, "gidx": gi,
    }


def finalize_for_run(nc):
    mask = {}
    for lib in all_libraries:
        for it in lib.instructions:
            mask[it] = mask.get(it, 0) | (1 << lib.index)
    _bass_rust.insert_library_loads(nc, mask, len(all_libraries), _std_lib.index)
    mybir.codegen_inst_isa_subclasses(nc)


def kernel(L0_attn_x, L0_attn_W_qkv, L0_attn_W_out, L0_ffn_wg, L0_ffn_fc1_w,
           L0_ffn_fc1_b, L0_ffn_fc2_w, L0_ffn_fc2_b, L0_ffn_ln_gamma,
           L0_ffn_ln_beta):
    cfg = dict(FULL_CFG); cfg['p2_early'] = False
    B, S, D = 4, 4096, 1024
    TOK = cfg["TOK"]
    x = np.asarray(L0_attn_x, np.float32)
    nc = build_program(cfg)
    finalize_for_run(nc)
    in_maps = []
    for c in range(8):
        b, h = c // 2, c % 2
        x_own = x[b, h * TOK:(h + 1) * TOK, :]
        x_oth = x[b, (1 - h) * TOK:(2 - h) * TOK, :]
        in_maps.append(_host_prep(
            cfg, x_own, x_oth, np.asarray(L0_attn_W_qkv), np.asarray(L0_attn_W_out),
            np.asarray(L0_ffn_wg), np.asarray(L0_ffn_fc1_w), np.asarray(L0_ffn_fc1_b),
            np.asarray(L0_ffn_fc2_w), np.asarray(L0_ffn_fc2_b),
            np.asarray(L0_ffn_ln_gamma), np.asarray(L0_ffn_ln_beta), c))
    res = run_bass_kernel_spmd(nc, in_maps, core_ids=list(range(8)))
    outs = [np.asarray(r["out"]) for r in res.results]
    return np.concatenate(outs, 0).reshape(B, S, D).astype(np.float32)


# version 41
# speedup vs baseline: 1.0674x; 1.0160x over previous
import math
import numpy as np
import ml_dtypes
from contextlib import ExitStack
import concourse.bass as bass
import concourse.mybir as mybir
import bass_rust as _bass_rust
from concourse.bass_utils import run_bass_kernel_spmd
from concourse.library_config import all_libraries, standard as _std_lib

F32 = mybir.dt.float32
BF16 = mybir.dt.bfloat16
U32 = mybir.dt.uint32
I16 = mybir.dt.int16
AX = mybir.AxisListType.X
AF = mybir.ActivationFunctionType
OP = mybir.AluOpType

FULL_CFG = dict(D=1024, E=8, H=4096, TOK=2048, SF=4096, PAD=640, CAP=5120,
                NCORES=8)


def build_program(cfg):
    D = cfg["D"]; E = cfg["E"]; H = cfg["H"]; TOK = cfg["TOK"]
    SF = cfg["SF"]; PAD = cfg["PAD"]; CAP = cfg["CAP"]; NCORES = cfg["NCORES"]
    assert E == 8
    DC = D // 128; HC = H // 128; NT = TOK // 128
    NBF = SF // 512; NOB = TOK // 512
    EPAD = E * PAD; NSL = PAD // 128; GIW = PAD // 16
    NQB = TOK // 512
    KC = SF // 128
    SCALE = 1.0 / math.sqrt(D)
    DW = min(D, 512); DHN = D // DW
    QH = 256
    SBW = PAD // 2 if PAD // 2 <= 512 else 512
    NSB = PAD // SBW
    HS = min(H, 1024); NHS = H // HS
    HSC = HS // 128

    nc = bass.Bass("TRN2", target_bir_lowering=False, debug=False)

    xo_h = nc.dram_tensor("x_own", [TOK, D], F32, kind="ExternalInput")
    xt_h = nc.dram_tensor("x_oth", [SF - TOK, D], F32, kind="ExternalInput")
    wqt_h = nc.dram_tensor("wq_t", [D, D], F32, kind="ExternalInput")
    wkt_h = nc.dram_tensor("wk_t", [D, D], F32, kind="ExternalInput")
    wvt_h = nc.dram_tensor("wv_t", [D, D], F32, kind="ExternalInput")
    wot_h = nc.dram_tensor("wo_t", [D, D], F32, kind="ExternalInput")
    wg_h = nc.dram_tensor("wg", [D, E], F32, kind="ExternalInput")
    fc1s_h = nc.dram_tensor("fc1s", [E * (H // min(H, 1024)) * 128,
                                     (D // 128) * min(H, 1024)], F32,
                            kind="ExternalInput")
    fc2r_h = nc.dram_tensor("fc2r", [E * 128, (H // 128) * D], F32,
                            kind="ExternalInput")
    fc1b_h = nc.dram_tensor("fc1b", [E, H], F32, kind="ExternalInput")
    fc2b_h = nc.dram_tensor("fc2b", [E, D], F32, kind="ExternalInput")
    lng_h = nc.dram_tensor("ln_g", [1, D], F32, kind="ExternalInput")
    lnb_h = nc.dram_tensor("ln_b", [1, D], F32, kind="ExternalInput")
    tri_h = nc.dram_tensor("tri", [128, 128], F32, kind="ExternalInput")
    ident_h = nc.dram_tensor("ident", [128, 128], F32, kind="ExternalInput")
    e8_h = nc.dram_tensor("e8", [128, E], F32, kind="ExternalInput")
    epad_h = nc.dram_tensor("epad_bc", [128, 2 * E], F32, kind="ExternalInput")
    pmask_h = nc.dram_tensor("pmask", [1, 16 * NCORES], F32, kind="ExternalInput")
    gidx_h = nc.dram_tensor("gidx", [128, E * GIW], I16, kind="ExternalInput")
    out_h = nc.dram_tensor("out", [TOK, D], F32, kind="ExternalOutput")

    disp_d = nc.dram_tensor("disp_d", [EPAD, D], BF16)
    ybuf_d = nc.dram_tensor("ybuf_d", [EPAD, D], BF16)
    xnsp_d = nc.dram_tensor("xnsp_d", [TOK, D], BF16)
    qsp_d = nc.dram_tensor("qsp_d", [DC, 128, TOK], BF16)
    aosp_d = nc.dram_tensor("aosp_d", [TOK, D], BF16)
    cci_d = nc.dram_tensor("cci_d", [1, 2 * E], F32)
    cco_d = nc.dram_tensor("cco_d", [NCORES, 2 * E], F32)

    stk = ExitStack()
    stkA = ExitStack()
    stkB = ExitStack()
    stkC = ExitStack()
    SB = lambda n, s, d: stk.enter_context(nc.sbuf_tensor(n, s, d))
    SBA = lambda n, s, d: stkA.enter_context(nc.sbuf_tensor(n, s, d))
    SBB = lambda n, s, d: stkB.enter_context(nc.sbuf_tensor(n, s, d))
    SBC = lambda n, s, d: stkC.enter_context(nc.sbuf_tensor(n, s, d))
    PS = lambda n, s, d: stk.enter_context(nc.psum_tensor(n, s, d))

    class Sem:

        def __init__(self, name):
            self.h = stk.enter_context(nc.semaphore(name)); self.n = 0

        def inc(self, instr, k=1):
            instr.then_inc(self.h, k); self.n += k; return self.n

        def dma(self, instr):
            return self.inc(instr, 16)

    vc = Sem("vc"); acx = Sem("acx"); tcx = Sem("tcx"); gcc = Sem("gcc")
    ccs = Sem("ccs")
    s_cst = Sem("s_cst"); s_pre = Sem("s_pre"); s_ccl = Sem("s_ccl")
    s_wol = Sem("s_wol")
    s_bld = Sem("s_bld")
    s_xld = [Sem(f"s_xld{i}") for i in range(2)]
    s_xnw = [Sem(f"s_xnw{i}") for i in range(2)]
    s_qst = [Sem(f"s_qst{i}") for i in range(2)]
    s_qsl = [Sem(f"s_qsl{i}") for i in range(2)]
    s_aos = [Sem(f"s_aos{i}") for i in range(2)]
    s_yst = [Sem(f"s_yst{i}") for i in range(2)]
    s_x5a = [Sem(f"s_x5a{i}") for i in range(2)]
    s_x5c = [Sem(f"s_x5c{i}") for i in range(2)]
    s_ost = [Sem(f"s_ost{i}") for i in range(2)]
    s_xrl = [Sem(f"s_xrl{i}") for i in range(2)]
    s_xsc = [Sem(f"s_xsc{i}") for i in range(2)]
    g_cst = Sem("g_cst"); g_cci = Sem("g_cci")
    g_w3p = [Sem(f"g_w3p{i}") for i in range(4)]
    g_f2q = [Sem(f"g_f2q{i}") for i in range(4)]
    g_sct = [Sem(f"g_sct{i}") for i in range(2)]
    g_xsl = Sem("g_xsl")
    g_xld = [Sem(f"g_xld{i}") for i in range(2)]
    g_gth = [Sem(f"g_gth{i}") for i in range(2)]
    g_f1l = [Sem(f"g_f1l{i}") for i in range(2)]
    g_ygl = [Sem(f"g_ygl{i}") for i in range(2)]

    ident = SB("ident_sb", [128, 128], BF16)
    onesc = SB("onesc", [128, 1], BF16)
    gates = SB("gates_sb", [128, NT * 2], F32)
    slots = SB("slots_sb", [128, NT * 2], U32)
    sums = SB("sums_sb", [128, NT], F32)
    mvst = SB("mvst", [128, 16], F32)
    xns = [SBB("xns0", [128, D], BF16)]
    kf = SBB("kf", [128, DC * SF], BF16)
    vt = SBB("vt", [128, KC * D], BF16)
    onesr = SBA("onesr", [1, 128], F32)
    tri = SBA("tri_sb", [128, 128], BF16)
    e8 = SBA("e8_sb", [128, E], F32)
    epad = SBA("epad_sb", [128, 2 * E], F32)
    pmask = SBA("pmask_sb", [1, 16 * NCORES], F32)
    g0buf = SBA("g0buf", [128, NT * 2], F32)
    masksf = SBA("masksf", [128, NT * 16], BF16)
    totrow = SBA("totrow", [1, 16 * (NT + 1)], F32)
    offrow = SBA("offrow", [1, 16 * NT], F32)
    rows48 = SBA("rows48", [1, 48 * NT], F32)
    cntloc = SBA("cntloc", [1, 16], F32)
    ccrow = SBA("ccrow", [1, 16 * NCORES], F32)
    masked = SBA("maskedcc", [1, 16 * NCORES], F32)
    rowa = SBA("rowa", [1, 16], F32)
    logt = SBA("logt", [128, NT * E], BF16)
    top8 = SBA("top8_sb", [128, 8], F32)
    topi = SBA("topi_sb", [128, 8], U32)
    rkA = SBA("rkA", [128, (NT // 4) * 48], F32)
    tA = SBA("tA", [128, (NT // 4) * 16], F32)
    k2a = SBA("k2a", [128, NT * 2], F32)
    idxall = SBA("idxall", [128, 2 * NT], F32)
    g0a = SBA("g0a", [128, NT], F32)
    g0s = SBA("g0s", [128, NT], F32)

    ps0 = PS("ps0", [128, 512], F32)
    ps1 = PS("ps1", [128, 512], F32)
    ps2 = PS("ps2", [128, 512], F32)
    ps3 = PS("ps3", [128, 512], F32)
    pst0 = PS("pst0", [128, 1024], BF16)
    pst1 = PS("pst1", [128, 1024], BF16)
    psr = PS("psr", [128, 512], F32)
    psw = PS("psw", [128, 512], F32)
    pspool = [ps0, ps1, ps2, ps3]

    st = {}

    def W(eng, ev, default=None):
        v = st.get(ev, default)
        if v is None:
            return
        eng.wait_ge(v[0].h, v[1])

    zsrc = kf[:, DC * SF - 512: DC * SF]
    nc.gpsimd.memset(zsrc, 0)
    gcc.inc(nc.gpsimd.engine_nop(), 1)
    g_cst.dma(nc.gpsimd.dma_start(out=tri[:, :], in_=tri_h[:, :]))
    g_cst.dma(nc.gpsimd.dma_start(out=ident[:, :], in_=ident_h[:, :]))
    st[("ident",)] = (g_cst, g_cst.n)

    s_cst.dma(nc.sync.dma_start(out=e8[:, :], in_=e8_h[:, :]))
    s_cst.dma(nc.sync.dma_start(out=epad[:, :], in_=epad_h[:, :]))
    s_cst.dma(nc.sync.dma_start(out=pmask[:, :], in_=pmask_h[:, :]))
    SQ_CONST = s_cst.n

    nc.vector.memset(onesr[:, :], 1.0)
    nc.vector.memset(onesc[:, :], 1.0)
    nc.vector.memset(totrow[:, :], 0.0)
    nc.vector.memset(offrow[:, :], 0.0)
    vc.inc(nc.vector.memset(rows48[:, :], 0.0), 1)
    VC_CONST = vc.n

    nc.sync.wait_ge(gcc.h, 1)
    pre_jobs = [(t, hv) for t in range(EPAD // 128)
                for hv in range((D + 511) // 512)]

    def emit_prefill_chunk(k, nchunks):
        per = -(-len(pre_jobs) // nchunks)
        for t, hv in pre_jobs[k * per:(k + 1) * per]:
            w_ = min(512, D - hv * 512)
            s_pre.dma(nc.sync.dma_start(
                out=disp_d[t * 128:(t + 1) * 128, hv * 512:hv * 512 + w_],
                in_=zsrc[:, 0:w_]))

    w3t = SBA("w3t", [128, 3 * DC * D], BF16)
    wgsb = SBA("wgsb", [128, DC * E], BF16)
    xtile = [SBA(f"xtile{i}", [128, D], BF16) for i in range(2)]
    xnf = [SBA("xnf0", [128, DC * 512], BF16)]
    xnt = [SBA(f"xnt{i}", [128, D], BF16) for i in range(2)]
    qevb = [SBA("qevb0", [128, 512], BF16), None]
    qevb1 = SBA("qevb1", [128, 512], BF16)
    logp = SBA("logp", [128, 512], BF16)

    w3_pieces = [(1, 0), (1, 1), (0, 0), (0, 1), (3, 0), (2, 0), (2, 1)]
    w3_keys = {1: ("w3k",), 0: ("w3q",), 2: ("w3v",)}

    def emit_w3_piece(k):
        if k >= len(w3_pieces):
            return
        wi, hh = w3_pieces[k]
        sem = g_w3p[wi]
        if wi == 3:
            sem.dma(nc.gpsimd.dma_start(
                out=wgsb[:, :].rearrange("p (c e) -> p c e", c=DC),
                in_=bass.AP(wg_h, 0, [[E, 128], [128 * E, DC], [1, E]])))
            st[("wg",)] = (sem, sem.n)
            return
        wh = [wqt_h, wkt_h, wvt_h][wi]
        DH2 = DC // 2
        sem.dma(nc.gpsimd.dma_start(
            out=w3t[:, wi * DC * D + hh * DH2 * D:
                    wi * DC * D + (hh + 1) * DH2 * D].rearrange(
                "p (c o) -> p c o", c=DH2),
            in_=bass.AP(wh, hh * DH2 * 128 * D, [[D, 128], [128 * D, DH2], [1, D]])))
        if hh == 1:
            st[w3_keys[wi]] = (sem, sem.n)

    blocks = [(xo_h, tb * 512, True) if tb < NOB else
              (xt_h, (tb - NOB) * 512, False) for tb in range(NBF)]

    def emit_p2a1(lo=0, hi=None):
        hi = NT if hi is None else hi
        if lo == 0:
            nc.vector.wait_ge(s_cst.h, SQ_CONST)
        for t in range(lo, hi):
            W(nc.vector, ("logt", t))
            vc.inc(nc.vector.max(top8[:, :], logt[:, t * E:(t + 1) * E]), 1)
            nc.vector.wait_ge(vc.h, vc.n)
            vc.inc(nc.vector.max_index(topi[:, :], top8[:, :],
                                       logt[:, t * E:(t + 1) * E]), 1)
            nc.vector.wait_ge(vc.h, vc.n)
            vc.inc(nc.vector.tensor_copy(idxall[:, 2 * t:2 * t + 2],
                                         topi[:, 0:2]), 1)
            vc.inc(nc.vector.tensor_sub(g0a[:, t:t + 1], top8[:, 0:1],
                                        top8[:, 1:2]), 1)
            nc.vector.wait_ge(vc.h, vc.n)
        st[('p2a1_hi',)] = hi

    def emit_p2a1b():
        nc.scalar.wait_ge(vc.h, vc.n)
        acx.inc(nc.scalar.activation(g0s[:, :], g0a[:, :], AF.Sigmoid), 1)
        nc.vector.wait_ge(acx.h, acx.n)
        vc.inc(nc.vector.tensor_copy(
            bass.AP(g0buf, 0, [[2 * NT, 128], [2, NT]]), g0s[:, :]), 1)
        vc.inc(nc.vector.tensor_scalar(
            bass.AP(g0buf, 1, [[2 * NT, 128], [2, NT]]), g0s[:, :], -1.0, 1.0,
            op0=OP.mult, op1=OP.add), 1)
        nc.vector.wait_ge(vc.h, vc.n)
        for t in range(NT):
            for j in range(2):
                vc.inc(nc.vector.tensor_scalar(
                    masksf[:, t * 16 + j * 8: t * 16 + (j + 1) * 8], e8[:, :],
                    idxall[:, 2 * t + j:2 * t + j + 1], None, op0=OP.is_equal), 1)
            st[("mask", t)] = (vc, vc.n)
        st[('p2a1b_done',)] = True

    def emit_p2a2():
        W(nc.tensor, ("mask", NT - 1))
        W(nc.tensor, ("pswfree2",), st.get(("pswfree",)))
        mm = nc.tensor.matmul(psw[0:1, 0:NT * 16], onesc[:, 0:1],
                              masksf[:, 0:NT * 16], start=True, stop=True)
        tcx.inc(mm, 1)
        nc.vector.wait_ge(tcx.h, tcx.n)
        vc.inc(nc.vector.tensor_copy(totrow[:, 16:16 + NT * 16],
                                     psw[0:1, 0:NT * 16]), 1)
        st[("pswfree2",)] = (vc, vc.n)
        for t in range(NT):
            if t == 0:
                vc.inc(nc.vector.tensor_copy(offrow[:, 0:16], totrow[:, 0:16]), 1)
            else:
                vc.inc(nc.vector.tensor_add(
                    offrow[:, t * 16:(t + 1) * 16],
                    offrow[:, (t - 1) * 16: t * 16],
                    totrow[:, t * 16:(t + 1) * 16]), 1)
            nc.vector.wait_ge(vc.h, vc.n)
        vc.inc(nc.vector.tensor_add(cntloc[:, :],
                                    offrow[:, (NT - 1) * 16: NT * 16],
                                    totrow[:, NT * 16:(NT + 1) * 16]), 1)
        st[("cnt",)] = (vc, vc.n)

        st[('p2a2_done',)] = True

    def emit_p2a2c():
        W(nc.gpsimd, ("cnt",))
        g_cci.dma(nc.gpsimd.dma_start(out=cci_d[:, :], in_=cntloc[:, :]))
        nc.gpsimd.wait_ge(g_cci.h, g_cci.n)
        ccs.inc(nc.gpsimd.collective_compute(
            "AllGather", OP.bypass, replica_groups=[list(range(NCORES))],
            ins=[cci_d.ap().opt()], outs=[cco_d.ap().opt()]), 1)
        st[('p2a2c_done',)] = True

    def emit_p2bh():
        if st.get(('p2a1_hi',), 0) < NT:
            emit_p2a1(st.get(('p2a1_hi',), 0), NT)
        if ('p2a1b_done',) not in st:
            emit_p2a1b()
        if ('p2a2_done',) not in st:
            emit_p2a2()
        if ('p2a2c_done',) not in st:
            emit_p2a2c()
        nc.sync.wait_ge(ccs.h, ccs.n)
        s_ccl.dma(nc.sync.dma_start(
            out=ccrow[0:1, :],
            in_=bass.AP(cco_d, 0, [[16 * NCORES, 1], [1, 16 * NCORES]])))

        nc.vector.wait_ge(s_ccl.h, s_ccl.n)
        vc.inc(nc.vector.tensor_mul(masked[:, :], ccrow[:, :], pmask[:, :]), 1)
        nc.vector.wait_ge(vc.h, vc.n)
        vc.inc(nc.vector.tensor_reduce(
            rowa[:, 0:16],
            bass.AP(masked, 0, [[16 * NCORES, 1], [1, 16], [16, NCORES]]),
            axis=AX, op=OP.add), 1)
        nc.vector.wait_ge(vc.h, vc.n)
        vc.inc(nc.vector.tensor_reduce(
            mvst[0:1, 0:E],
            bass.AP(ccrow, 0, [[16 * NCORES, 1], [1, E], [16, NCORES]]),
            axis=AX, op=OP.add), 1)
        nc.vector.wait_ge(vc.h, vc.n)
        vc.inc(nc.vector.tensor_add(rowa[:, E:16], rowa[:, E:16],
                                    mvst[0:1, 0:E]), 1)
        nc.vector.wait_ge(vc.h, vc.n)
        vc.inc(nc.vector.tensor_copy(
            bass.AP(rows48, 0, [[48 * NT, 1], [48, NT], [1, 16]]),
            bass.AP(offrow, 0, [[16 * NT, 1], [16, NT], [1, 16]])), 1)
        vc.inc(nc.vector.tensor_copy(totrow[:, 0:16], epad[0:1, :]), 1)
        nc.vector.wait_ge(vc.h, vc.n)
        vc.inc(nc.vector.tensor_add(totrow[:, 8:16], totrow[:, 8:16],
                                    cntloc[:, 0:E]), 1)
        nc.vector.wait_ge(vc.h, vc.n)
        for t in range(NT):
            vc.inc(nc.vector.tensor_copy(rows48[:, t * 48 + 16: t * 48 + 32],
                                         rowa[:, 0:16]), 1)
            vc.inc(nc.vector.tensor_copy(rows48[:, t * 48 + 32: t * 48 + 48],
                                         totrow[:, 0:16]), 1)
        nc.vector.wait_ge(vc.h, vc.n)
        st[('rows_ok',)] = vc.n
        st[('p2bh_done',)] = True

    def emit_p2b():
        if ('p2bh_done',) not in st:
            emit_p2bh()
        ROWS_OK = st[('rows_ok',)]

        NTH = NT // 4
        for hb in range(4):
            for tt in range(NTH):
                t = hb * NTH + tt
                nc.tensor.wait_ge(vc.h, ROWS_OK)
                W(nc.tensor, ("psrfree",))
                if hb:
                    W(nc.tensor, ("rkAfree",))
                nc.tensor.matmul(psr[:, 0:48], onesr[0:1, 0:128],
                                 rows48[:, t * 48:(t + 1) * 48], start=True, stop=False)
                mm = nc.tensor.matmul(psr[:, 0:16], tri[:, :],
                                      masksf[:, t * 16:(t + 1) * 16],
                                      start=False, stop=True)
                tcx.inc(mm, 1)
                nc.vector.wait_ge(tcx.h, tcx.n)
                vc.inc(nc.vector.tensor_copy(rkA[:, tt * 48:(tt + 1) * 48],
                                             psr[:, 0:48]), 1)
                st[("psrfree",)] = (vc, vc.n)
            nc.vector.wait_ge(vc.h, vc.n)
            v3 = lambda off: bass.AP(rkA, off, [[NTH * 48, 128], [48, NTH], [1, 16]])
            tAv = bass.AP(tA, 0, [[NTH * 16, 128], [16, NTH], [1, 16]])
            mfh = masksf[:, hb * NTH * 16:(hb + 1) * NTH * 16]
            vc.inc(nc.vector.tensor_tensor(tAv, v3(0), v3(16), op=OP.add), 1)
            nc.vector.wait_ge(vc.h, vc.n)
            vc.inc(nc.vector.tensor_single_scalar(tA[:, :], tA[:, :],
                                                  float(CAP), OP.is_lt), 1)
            nc.vector.wait_ge(vc.h, vc.n)
            vc.inc(nc.vector.tensor_mul(tA[:, :], tA[:, :], mfh), 1)
            nc.vector.wait_ge(vc.h, vc.n)
            vc.inc(nc.vector.tensor_reduce(
                gates[:, hb * NTH * 2:(hb + 1) * NTH * 2],
                bass.AP(tA, 0, [[NTH * 16, 128], [8, NTH * 2], [1, 8]]),
                axis=AX, op=OP.add), 1)
            nc.vector.wait_ge(vc.h, vc.n)
            vc.inc(nc.vector.tensor_mul(gates[:, hb * NTH * 2:(hb + 1) * NTH * 2],
                                        gates[:, hb * NTH * 2:(hb + 1) * NTH * 2],
                                        g0buf[:, hb * NTH * 2:(hb + 1) * NTH * 2]), 1)
            vc.inc(nc.vector.tensor_tensor(tAv, v3(0), v3(32), op=OP.add), 1)
            nc.vector.wait_ge(vc.h, vc.n)
            vc.inc(nc.vector.tensor_mul(tA[:, :], tA[:, :], mfh), 1)
            nc.vector.wait_ge(vc.h, vc.n)
            vc.inc(nc.vector.tensor_reduce(
                k2a[:, hb * NTH * 2:(hb + 1) * NTH * 2],
                bass.AP(tA, 0, [[NTH * 16, 128], [8, NTH * 2], [1, 8]]),
                axis=AX, op=OP.add), 1)
            nc.vector.wait_ge(vc.h, vc.n)
            st[("rkAfree",)] = (vc, vc.n)
            vc.inc(nc.vector.tensor_scalar_min(
                k2a[:, hb * NTH * 2:(hb + 1) * NTH * 2],
                k2a[:, hb * NTH * 2:(hb + 1) * NTH * 2], float(EPAD - 1)), 1)
            nc.vector.wait_ge(vc.h, vc.n)
            vc.inc(nc.vector.tensor_copy(slots[:, hb * NTH * 2:(hb + 1) * NTH * 2],
                                         k2a[:, hb * NTH * 2:(hb + 1) * NTH * 2]), 1)
            nc.vector.wait_ge(vc.h, vc.n)
        SLOTS_OK = vc.n
        for t in range(NT):
            st[("slot", t)] = (vc, SLOTS_OK)
        st[('P2_VC',)] = vc.n

        st[('p2_done',)] = True

    def emit_scatters():
        for t in range(NT):
            xs = 0
            W(nc.gpsimd, ("xns_scat", xs))
            W(nc.gpsimd, ("xnsp", t))
            g_xsl.dma(nc.gpsimd.dma_start(out=xns[xs][:, :],
                                          in_=xnsp_d[t * 128:(t + 1) * 128, :]))
            st[("xnrl", t)] = (g_xsl, g_xsl.n)
            W(nc.gpsimd, ("xnrl", t))
            W(nc.gpsimd, ("slot", t))
            if t == 0:
                nc.gpsimd.wait_ge(s_pre.h, s_pre.n)
            for j in range(2):
                g_sct[xs].dma(nc.gpsimd.indirect_dma_start(
                    out=disp_d[:, :],
                    out_offset=bass.IndirectOffsetOnAxis(
                        ap=slots[:, t * 2 + j: t * 2 + j + 1], axis=0),
                    in_=xns[xs][:, :], in_offset=None))
            st[("xns_scat", xs)] = (g_sct[xs], g_sct[xs].n)


    NGTS = 4 * len(blocks)
    fslot = 0

    def emit_ln(gt):
        src_h, roff, own = blocks[gt // 4]
        i = gt % 4
        xs = gt % 2
        W(nc.gpsimd, ("xtile_free", xs))
        g_xld[xs].dma(nc.gpsimd.dma_start(
            out=xtile[xs][:, :],
            in_=src_h[roff + i * 128: roff + (i + 1) * 128, :]))
        st[("xload", gt)] = (g_xld[xs], g_xld[xs].n)

        W(nc.vector, ("xload", gt))
        nst = (D + 511) // 512
        for c_ in range(nst):
            w_ = min(512, D - c_ * 512)
            vc.inc(nc.vector.bn_stats(
                mvst[:, 6 * c_:6 * c_ + 6],
                xtile[xs][:, c_ * 512: c_ * 512 + w_]), 1)
            nc.vector.wait_ge(vc.h, vc.n)
        vc.inc(nc.vector.bn_aggr(mvst[:, 12:14], mvst[:, 0:6 * nst]), 1)
        nc.vector.wait_ge(vc.h, vc.n)
        vc.inc(nc.vector.tensor_scalar_add(mvst[:, 14:15], mvst[:, 13:14], 1e-5), 1)
        nc.scalar.wait_ge(vc.h, vc.n)
        acx.inc(nc.scalar.activation(mvst[:, 15:16], mvst[:, 14:15], AF.Sqrt), 1)
        nc.vector.wait_ge(acx.h, acx.n)
        vc.inc(nc.vector.reciprocal(mvst[:, 15:16], mvst[:, 15:16]), 1)
        nc.vector.wait_ge(vc.h, vc.n)
        xslot = gt % 2
        W(nc.vector, ("xnt_free", xslot))
        W(nc.vector, ("xnt_spill", xslot))
        vc.inc(nc.vector.tensor_scalar(
            xnt[xslot][:, :], xtile[xs][:, :], mvst[:, 12:13], mvst[:, 15:16],
            op0=OP.subtract, op1=OP.mult), 1)
        st[("xn", gt)] = (vc, vc.n)
        st[("xtile_free", xs)] = (vc, vc.n)

        if own:
            W(nc.sync, ("xn", gt))
            s_xnw[xslot].dma(nc.sync.dma_start(
                out=xnsp_d[gt * 128:(gt + 1) * 128, :], in_=xnt[xslot][:, :]))
            st[("xnt_spill", xslot)] = (s_xnw[xslot], s_xnw[xslot].n)
            st[("xnsp", gt)] = (s_xnw[xslot], s_xnw[xslot].n)

    for tb, (src_h, roff, own) in enumerate(blocks):
        if tb >= 1:
            emit_prefill_chunk(tb - 1, max(1, len(blocks) - 1))
        if cfg.get("p2a_early", True) and tb == NOB + 2:
            if st.get(('p2a1_hi',), 0) < NT:
                emit_p2a1(st.get(('p2a1_hi',), 0), NT)
            emit_p2a1b()
            emit_p2a2()
        for i in range(4):
            gt = tb * 4 + i
            xslot = gt % 2
            if gt == 0:
                emit_ln(0)
                emit_ln(1)
                W(nc.tensor, ("ident",))

            W(nc.tensor, ("xn", gt))
            pstile = [pst0, pst1][gt % 2]
            W(nc.tensor, ("trev", gt % 2))
            for dcp in range(DC):
                tcx.inc(nc.tensor.transpose(
                    pstile[:, dcp * 128:(dcp + 1) * 128],
                    xnt[xslot][:, dcp * 128:(dcp + 1) * 128],
                    ident[:, :]), 1)
            st[("tr", gt)] = (tcx, tcx.n)
            W(nc.vector, ("tr", gt))
            vc.inc(nc.vector.tensor_copy(
                bass.AP(xnf[fslot], i * 128, [[DC * 512, 128], [512, DC], [1, 128]]),
                pstile[:, 0:DC * 128].rearrange("p (c k) -> p c k", c=DC)), 1)
            st[("trev", gt % 2)] = (vc, vc.n)
            st[("xnt_free", xslot)] = (tcx, tcx.n)
            st[("xnf", tb, i)] = (vc, vc.n)
            if gt + 2 < NGTS:
                emit_ln(gt + 2)
            if gt < 4:
                emit_w3_piece(2 * gt)
                emit_w3_piece(2 * gt + 1)

            if cfg.get("p2a_early", True):
                q4 = NT // 4 if NT >= 4 else 0
                if q4 and tb == NOB + 1:
                    emit_p2a1(i * q4, (i + 1) * q4)
                if tb == NOB + 2 and i == 0 and ('p2a2_done',) in st:
                    emit_p2a2c()

        for i in range(4):
            W(nc.tensor, ("xnf", tb, i))
        pi = 0
        wlist = [(1, "k")] + ([(0, "q")] if own else [])
        for wsel, nm in wlist:
            W(nc.tensor, ("w3k",) if nm == "k" else ("w3q",))
            for oc in range(DC):
                p = pspool[pi % 4]; pi += 1
                W(nc.tensor, ("psfree", p.name))
                for hc in range(DC):
                    mm = nc.tensor.matmul(
                        p[:, 0:512],
                        w3t[:, wsel * DC * D + hc * D + oc * 128:
                            wsel * DC * D + hc * D + oc * 128 + 128],
                        xnf[fslot][:, hc * 512:(hc + 1) * 512],
                        start=(hc == 0), stop=(hc == DC - 1))
                tcx.inc(mm, 1)
                nc.vector.wait_ge(tcx.h, tcx.n)
                if nm == "k":
                    if oc == DC - 1 and tb == NBF - 1:
                        nc.vector.wait_ge(s_pre.h, s_pre.n)
                    vc.inc(nc.vector.tensor_copy(
                        kf[:, oc * SF + tb * 512: oc * SF + (tb + 1) * 512],
                        p[:, 0:512]), 1)
                else:
                    qb_ = oc % 2
                    qt_ = qevb[0] if qb_ == 0 else qevb1
                    W(nc.vector, ("qevb_used", qb_))
                    vc.inc(nc.vector.tensor_copy(qt_[:, :], p[:, 0:512]), 1)
                    nc.sync.wait_ge(vc.h, vc.n)
                    s_qst[qb_].dma(nc.sync.dma_start(
                        out=qsp_d[oc, :, tb * 512:(tb + 1) * 512],
                        in_=qt_[:, :]))
                    st[("qevb_used", qb_)] = (s_qst[qb_], s_qst[qb_].n)
                st[("psfree", p.name)] = (vc, vc.n)
        W(nc.tensor, ("w3v",))
        for tc4 in range(4):
            for dh in range(DHN):
                p = pspool[pi % 4]; pi += 1
                W(nc.tensor, ("psfree", p.name))
                for hc in range(DC):
                    mm = nc.tensor.matmul(
                        p[:, 0:DW],
                        xnf[fslot][:, hc * 512 + tc4 * 128: hc * 512 + (tc4 + 1) * 128],
                        w3t[:, 2 * DC * D + hc * D + dh * DW:
                            2 * DC * D + hc * D + (dh + 1) * DW],
                        start=(hc == 0), stop=(hc == DC - 1))
                tcx.inc(mm, 1)
                kc = tb * 4 + tc4
                nc.vector.wait_ge(tcx.h, tcx.n)
                vc.inc(nc.vector.tensor_copy(
                    vt[:, kc * D + dh * DW: kc * D + (dh + 1) * DW],
                    p[:, 0:DW]), 1)
                st[("psfree", p.name)] = (vc, vc.n)
        if own:
            W(nc.tensor, ("wg",))
            W(nc.tensor, ("pswfree",))
            for hc in range(DC):
                mm = nc.tensor.matmul(
                    psw[0:E, 0:512], wgsb[:, hc * E:(hc + 1) * E],
                    xnf[fslot][:, hc * 512:(hc + 1) * 512],
                    start=(hc == 0), stop=(hc == DC - 1))
            tcx.inc(mm, 1)
            nc.vector.wait_ge(tcx.h, tcx.n)
            vc.inc(nc.vector.tensor_copy(logp[0:E, :], psw[0:E, 0:512]), 1)
            st[("pswfree",)] = (vc, vc.n)
            st[("logev", tb)] = (vc, vc.n)
            W(nc.tensor, ("logev", tb))
            for i in range(4):
                gt = tb * 4 + i
                pstile = [pst0, pst1][i % 2]
                W(nc.tensor, ("trev", i % 2))
                tcx.inc(nc.tensor.transpose(
                    pstile[0:128, 0:E], logp[0:E, i * 128:(i + 1) * 128],
                    ident[0:E, 0:E]), 1)
                st[("ltr", gt)] = (tcx, tcx.n)
                W(nc.vector, ("ltr", gt))
                vc.inc(nc.vector.tensor_copy(
                    logt[:, gt * E:(gt + 1) * E], pstile[0:128, 0:E]), 1)
                st[("trev", i % 2)] = (vc, vc.n)
                st[("logt", gt)] = (vc, vc.n)
        st[("xnf_free", fslot)] = (tcx, tcx.n)

    emit_p2b()
    P1_TC = tcx.n
    P1_VC = vc.n
    stkA.close()

    wot = SBB("wot", [128, DC * D], BF16)
    qtile = [SBB(f"qtile{i}", [128, DC * 128], BF16) for i in range(2)]
    scoresb = SBB("scoresb", [128, SF], F32)
    probs = [SBB(f"probsb{i}", [128, SF], BF16) for i in range(2)]
    probsT = SBB("probsT", [128, KC * QH], BF16)
    ctxf = SBB("ctxf", [128, DC * QH], BF16)
    aot = [SBB("aot0", [128, 512], BF16)]
    NKC = SF // 512
    cmax = SBB("cmax", [128, 2 * NKC], F32)

    nc.gpsimd.wait_ge(vc.h, st[('P2_VC',)])
    s_wol.dma(nc.gpsimd.dma_start(
        out=wot[:, :].rearrange("p (c o) -> p c o", c=DC),
        in_=bass.AP(wot_h, 0, [[D, 128], [128 * D, DC], [1, D]])))
    emit_scatters()
    SCAT = [(g_sct[0], g_sct[0].n), (g_sct[1], g_sct[1].n)]

    nc.tensor.wait_ge(vc.h, P1_VC)

    def emit_scores(gt, qs):
        gp_ = gt % 2
        W(nc.tensor, ("qldt", gt))
        for kc8 in range(SF // 512):
            p = pspool[kc8 % 2]
            W(nc.tensor, ("psfree", p.name))
            for hc in range(DC):
                mm = nc.tensor.matmul(
                    p[:, 0:512],
                    qtile[qs][:, hc * 128:(hc + 1) * 128],
                    kf[:, hc * SF + kc8 * 512: hc * SF + (kc8 + 1) * 512],
                    start=(hc == 0), stop=(hc == DC - 1))
            tcx.inc(mm, 1)
            nc.vector.wait_ge(tcx.h, tcx.n)
            if kc8 == 0:
                W(nc.vector, ("exp_prev",))
            vc.inc(nc.vector.tensor_copy(
                scoresb[:, kc8 * 512:(kc8 + 1) * 512], p[:, 0:512]), 1)
            vc.inc(nc.vector.tensor_reduce(
                cmax[:, gp_ * NKC + kc8: gp_ * NKC + kc8 + 1],
                p[:, 0:512], axis=AX, op=OP.max), 1)
            st[("psfree", p.name)] = (vc, vc.n)
        st[("scev", gt)] = (vc, vc.n)
        st[("qtile_used", gt % 2)] = (tcx, tcx.n)

    def emit_softmax(gt):
        gp_ = gt % 2
        W(nc.vector, ("scev", gt))
        W(nc.vector, ("exp_prev2", gp_))
        vc.inc(nc.vector.reduce_max(mvst[:, 4 * gp_:4 * gp_ + 1],
                                    cmax[:, gp_ * NKC: gp_ * NKC + NKC], axis=AX), 1)
        nc.vector.wait_ge(vc.h, vc.n)
        vc.inc(nc.vector.tensor_scalar_mul(
            mvst[:, 4 * gp_ + 1:4 * gp_ + 2],
            mvst[:, 4 * gp_:4 * gp_ + 1], -SCALE), 1)
        nc.scalar.wait_ge(vc.h, vc.n)
        W(nc.scalar, ("ptr_last", gp_))
        acx.inc(nc.scalar.activation(
            probs[gp_][:, :], scoresb[:, :], AF.Exp,
            bias=mvst[:, 4 * gp_ + 1:4 * gp_ + 2],
            scale=SCALE, accum_out=mvst[:, 4 * gp_ + 2:4 * gp_ + 3]), 1)
        st[("exp", gt)] = (acx, acx.n)
        st[("exp_prev",)] = (acx, acx.n)
        st[("exp_prev2", gp_)] = (acx, acx.n)
        W(nc.vector, ("exp", gt))
        vc.inc(nc.vector.reciprocal(sums[:, gt:gt + 1],
                                    mvst[:, 4 * gp_ + 2:4 * gp_ + 3]), 1)

    def emit_transposes(gt):
        gp_ = gt % 2
        ti = gt % 2
        W(nc.tensor, ("exp", gt))
        for kb in range(0, KC, 8):
            nb = min(8, KC - kb)
            pstile = [pst0, pst1][(kb // 8) % 2]
            W(nc.tensor, ("trev", (kb // 8) % 2))
            if kb == 0:
                W(nc.tensor, ("probsT_done",))
            for k2_ in range(nb):
                kc = kb + k2_
                tcx.inc(nc.tensor.transpose(
                    pstile[:, k2_ * 128:(k2_ + 1) * 128],
                    probs[gp_][:, kc * 128:(kc + 1) * 128],
                    ident[:, :]), 1)
            st[("ptr", gt, kb)] = (tcx, tcx.n)
            W(nc.vector, ("ptr", gt, kb))
            vc.inc(nc.vector.tensor_copy(
                bass.AP(probsT, kb * QH + ti * 128,
                        [[KC * QH, 128], [QH, nb], [1, 128]]),
                pstile[:, 0:nb * 128].rearrange("p (k c) -> p k c", k=nb)), 1)
            st[("trev", (kb // 8) % 2)] = (vc, vc.n)
        st[("ptr_done", gt)] = (vc, vc.n)
        st[("ptr_last", gp_)] = st[("ptr", gt, KC - 8 if KC >= 8 else 0)]

    def emit_ctx_ao(h):
        W(nc.tensor, ("ptr_done", 2 * h))
        W(nc.tensor, ("ptr_done", 2 * h + 1))
        for hc in range(DC):
            p = pspool[2 + (hc % 2)]
            W(nc.tensor, ("psfree", p.name))
            for kc in range(KC):
                mm = nc.tensor.matmul(
                    p[:, 0:QH],
                    vt[:, kc * D + hc * 128: kc * D + (hc + 1) * 128],
                    probsT[:, kc * QH:(kc + 1) * QH],
                    start=(kc == 0), stop=(kc == KC - 1))
            tcx.inc(mm, 1)
            nc.vector.wait_ge(tcx.h, tcx.n)
            vc.inc(nc.vector.tensor_copy(
                ctxf[:, hc * QH:(hc + 1) * QH], p[:, 0:QH]), 1)
            st[("psfree", p.name)] = (vc, vc.n)
        st[("ctx_done",)] = (vc, vc.n)
        st[("probsT_done",)] = (tcx, tcx.n)
        nc.tensor.wait_ge(s_wol.h, s_wol.n)
        W(nc.tensor, ("ctx_done",))
        for qc2 in range(2):
            gt = 2 * h + qc2
            asl = 0
            for dh in range(DHN):
                p = pspool[dh % 2]
                W(nc.tensor, ("psfree", p.name))
                for hc in range(DC):
                    mm = nc.tensor.matmul(
                        p[:, 0:DW],
                        ctxf[:, hc * QH + qc2 * 128: hc * QH + (qc2 + 1) * 128],
                        wot[:, hc * D + dh * DW: hc * D + (dh + 1) * DW],
                        start=(hc == 0), stop=(hc == DC - 1))
                tcx.inc(mm, 1)
                nc.vector.wait_ge(tcx.h, tcx.n)
                W(nc.vector, ("aot_used", asl))
                vc.inc(nc.vector.tensor_scalar_mul(
                    aot[asl][:, 0:DW], p[:, 0:DW],
                    sums[:, gt:gt + 1]), 1)
                st[("psfree", p.name)] = (vc, vc.n)
                nc.sync.wait_ge(vc.h, vc.n)
                s_aos[asl].dma(nc.sync.dma_start(
                    out=aosp_d[gt * 128:(gt + 1) * 128, dh * DW:(dh + 1) * DW],
                    in_=aot[asl][:, 0:DW]))
                st[("aot_used", asl)] = (s_aos[asl], s_aos[asl].n)

    def emit_qld(gt):
        qs = gt % 2
        W(nc.sync, ("qtile_used", qs))
        if gt < 2:
            nc.sync.wait_ge(s_qst[0].h, s_qst[0].n)
            nc.sync.wait_ge(s_qst[1].h, s_qst[1].n)
            nc.sync.wait_ge(vc.h, st[('P2_VC',)])
            nc.sync.wait_ge(tcx.h, P1_TC)
        s_qsl[qs].dma(nc.sync.dma_start(
            out=qtile[qs][:, :].rearrange("p (c q) -> p c q", c=DC),
            in_=bass.AP(qsp_d, gt * 128, [[TOK, 128], [128 * TOK, DC], [1, 128]])))
        st[("qldt", gt)] = (s_qsl[qs], s_qsl[qs].n)

    NGT = 4 * NQB
    prev = None
    for gt in range(NGT):
        if gt == 0:
            emit_qld(0)
            emit_qld(1)
        emit_scores(gt, gt % 2)
        if gt + 2 < NGT:
            emit_qld(gt + 2)
        if prev is not None:
            emit_transposes(prev)
        emit_softmax(gt)
        if prev is not None and prev % 2 == 1:
            emit_ctx_ao(prev // 2)
        prev = gt
    emit_transposes(prev)
    emit_ctx_ao(prev // 2)
    P3_TC = tcx.n
    P3_VC = vc.n
    P3_AC = acx.n
    stkB.close()

    gidx = SBC("gidx_sb", [128, E * GIW], I16)
    dispb = [SBC(f"dispb{i}", [128, DC * PAD], BF16) for i in range(2)]
    f1sl = [SBC(f"f1sl{i}", [128, DC * HS], BF16) for i in range(2)]
    f2sb = SBC("f2sb", [128, HC * D], BF16)
    hid = SBC("hid", [128, HC * PAD], BF16)
    f1bs = SBC("f1bs", [128, HC], F32)
    f2br = SBC("f2br", [1, D], F32)
    b2bc = SBC("b2bc", [128, D], F32)
    onesr4 = SBC("onesr4", [1, 128], F32)
    ytile = [SBC(f"ytile{i}", [128, D], BF16) for i in range(2)]

    for eng in [nc.gpsimd, nc.sync]:
        eng.wait_ge(tcx.h, P3_TC)
        eng.wait_ge(vc.h, P3_VC)
        eng.wait_ge(acx.h, P3_AC)
        eng.wait_ge(s_aos[0].h, s_aos[0].n)
        eng.wait_ge(s_aos[1].h, s_aos[1].n)
        eng.wait_ge(SCAT[0][0].h, SCAT[0][1])
        eng.wait_ge(SCAT[1][0].h, SCAT[1][1])
    s_cst.dma(nc.sync.dma_start(out=gidx[:, :], in_=gidx_h[:, :]))
    SQ_CONST = s_cst.n
    nc.gpsimd.wait_ge(s_cst.h, SQ_CONST)
    nc.gpsimd.wait_ge(SCAT[0][0].h, SCAT[0][1])
    nc.gpsimd.wait_ge(SCAT[1][0].h, SCAT[1][1])
    nc.vector.wait_ge(tcx.h, P3_TC)
    vc.inc(nc.vector.memset(onesr4[:, :], 1.0), 1)
    ONESR4_OK = vc.n
    nc.tensor.wait_ge(vc.h, ONESR4_OK)
    for e in range(E):
        ds = e % 2
        W(nc.gpsimd, ("disp_used", ds))
        g_gth[ds].dma(nc.gpsimd.dma_gather(
            dispb[ds][:, :].rearrange("p (c s) -> p c s", c=DC),
            disp_d[:, :], gidx[:, e * GIW:(e + 1) * GIW], PAD, PAD, D,
            transpose=True))
        st[("disp", e)] = (g_gth[ds], g_gth[ds].n)
        W(nc.sync, ("f1b_used",))
        W(nc.sync, ("f2b_used",))
        s_bld.dma(nc.sync.dma_start(
            out=f1bs[:, :], in_=bass.AP(fc1b_h, e * H, [[1, 128], [128, HC]]),
            allow_slow_non_contiguous=True))
        s_bld.dma(nc.sync.dma_start(out=f2br[:, :], in_=fc2b_h[e:e + 1, :]))
        st[("bld", e)] = (s_bld, s_bld.n)
        for hs in range(NHS):
            fs = hs % 2
            W(nc.gpsimd, ("f1sl_used", fs))
            g_f1l[fs].dma(nc.gpsimd.dma_start(
                out=f1sl[fs][:, :],
                in_=fc1s_h[(e * NHS + hs) * 128:(e * NHS + hs + 1) * 128, :]))
            st[("f1sl", e, hs)] = (g_f1l[fs], g_f1l[fs].n)
            HQ = HC // 4
            for q_ in range(4):
                if hs == min(1 + q_, NHS - 1) and ("f2q_em", e, q_) not in st:
                    st[("f2q_em", e, q_)] = True
                    qsem = g_f2q[q_]
                    W(nc.gpsimd, ("f2_used",))
                    qsem.dma(nc.gpsimd.dma_start(
                        out=f2sb[:, q_ * HQ * D:(q_ + 1) * HQ * D],
                        in_=fc2r_h[e * 128:(e + 1) * 128,
                                   q_ * HQ * D:(q_ + 1) * HQ * D]))
                    st[("f2q", e, q_)] = (qsem, qsem.n)
            W(nc.tensor, ("f1sl", e, hs))
            if hs == 0:
                W(nc.tensor, ("disp", e))
            for hcc in range(HSC):
                hhc = hs * HSC + hcc
                for sbk in range(NSB):
                    p = pspool[(hcc * NSB + sbk) % 4]
                    W(nc.tensor, ("psfree", p.name))
                    for dc in range(DC):
                        mm = nc.tensor.matmul(
                            p[:, 0:SBW],
                            f1sl[fs][:, dc * HS + hcc * 128: dc * HS + (hcc + 1) * 128],
                            dispb[ds][:, dc * PAD + sbk * SBW:
                                      dc * PAD + (sbk + 1) * SBW],
                            start=(dc == 0), stop=(dc == DC - 1))
                    tcx.inc(mm, 1)
                    nc.scalar.wait_ge(tcx.h, tcx.n)
                    W(nc.scalar, ("bld", e))
                    acx.inc(nc.scalar.activation(
                        hid[:, hhc * PAD + sbk * SBW: hhc * PAD + (sbk + 1) * SBW],
                        p[:, 0:SBW], AF.Silu, bias=f1bs[:, hhc:hhc + 1]), 1)
                    st[("psfree", p.name)] = (acx, acx.n)
            st[("f1sl_used", fs)] = (acx, acx.n)
        st[("disp_used", ds)] = (tcx, tcx.n)
        st[("f1b_used",)] = (acx, acx.n)
        HID_OK = acx.n


        nc.tensor.wait_ge(acx.h, HID_OK)
        W(nc.tensor, ("bld", e))
        p = pspool[2]
        for dh in range(DHN):
            W(nc.tensor, ("psfree", p.name))
            mm = nc.tensor.matmul(p[:, 0:DW], onesr4[0:1, 0:128],
                                  f2br[:, dh * DW:(dh + 1) * DW],
                                  start=True, stop=True)
            tcx.inc(mm, 1)
            nc.vector.wait_ge(tcx.h, tcx.n)
            W(nc.vector, ("b2bc_used",))
            vc.inc(nc.vector.tensor_copy(b2bc[:, dh * DW:(dh + 1) * DW],
                                         p[:, 0:DW]), 1)
            st[("psfree", p.name)] = (vc, vc.n)
        st[("b2bc",)] = (vc, vc.n)
        W(nc.vector, ("b2bc",))
        for sc in range(NSL):
            ys = sc % 2
            for dh in range(DHN):
                p = pspool[dh % 2]
                W(nc.tensor, ("psfree", p.name))
                for hhc in range(HC):
                    if sc == 0 and dh == 0 and hhc % (HC // 4) == 0:
                        W(nc.tensor, ("f2q", e, hhc // (HC // 4)))
                    mm = nc.tensor.matmul(
                        p[:, 0:DW],
                        hid[:, hhc * PAD + sc * 128: hhc * PAD + (sc + 1) * 128],
                        f2sb[:, hhc * D + dh * DW: hhc * D + (dh + 1) * DW],
                        start=(hhc == 0), stop=(hhc == HC - 1))
                tcx.inc(mm, 1)
                nc.vector.wait_ge(tcx.h, tcx.n)
                W(nc.vector, ("ytile_used", ys))
                vc.inc(nc.vector.tensor_tensor(
                    ytile[ys][:, dh * DW:(dh + 1) * DW], p[:, 0:DW],
                    b2bc[:, dh * DW:(dh + 1) * DW], op=OP.add), 1)
                st[("psfree", p.name)] = (vc, vc.n)
            st[("yev", e, sc)] = (vc, vc.n)
            W(nc.sync, ("yev", e, sc))
            s_yst[ys].dma(nc.sync.dma_start(
                out=ybuf_d[e * PAD + sc * 128: e * PAD + (sc + 1) * 128, :],
                in_=ytile[ys][:, :]))
            st[("ytile_used", ys)] = (s_yst[ys], s_yst[ys].n)
        st[("f2_used",)] = (tcx, tcx.n)
        st[("f2b_used",)] = (tcx, tcx.n)
        st[("b2bc_used",)] = (tcx, tcx.n)
    YST = [(s_yst[0], s_yst[0].n), (s_yst[1], s_yst[1].n)]
    P4_TC = tcx.n
    P4_VC = vc.n
    P4_AC = acx.n
    stkC.close()

    lngb = SB("lngb", [128, D], BF16)
    lnbb = SB("lnbb", [128, D], BF16)
    yg = [SB(f"yg{i}", [128, 2 * D], BF16) for i in range(2)]
    xno = [SB(f"xno{i}", [128, D], BF16) for i in range(2)]
    xrl = [SB(f"xrl{i}", [128, D], F32) for i in range(2)]
    aorl = [SB(f"aorl{i}", [128, D], BF16) for i in range(2)]
    rest_ = [SB(f"rest{i}", [128, D], F32) for i in range(2)]
    outt = [SB(f"outt{i}", [128, D], F32) for i in range(2)]

    for eng in [nc.gpsimd, nc.sync]:
        eng.wait_ge(YST[0][0].h, YST[0][1])
        eng.wait_ge(YST[1][0].h, YST[1][1])
        eng.wait_ge(tcx.h, P4_TC)
        eng.wait_ge(vc.h, P4_VC)
        eng.wait_ge(acx.h, P4_AC)
        for q_ in range(4):
            eng.wait_ge(g_f2q[q_].h, g_f2q[q_].n)
        eng.wait_ge(g_gth[0].h, g_gth[0].n)
        eng.wait_ge(g_gth[1].h, g_gth[1].n)
        eng.wait_ge(g_f1l[0].h, g_f1l[0].n)
        eng.wait_ge(g_f1l[1].h, g_f1l[1].n)
        eng.wait_ge(s_bld.h, s_bld.n)
    g_cst.dma(nc.gpsimd.dma_start(
        out=lngb[:, :], in_=bass.AP(lng_h, 0, [[0, 128], [1, D]])))
    g_cst.dma(nc.gpsimd.dma_start(
        out=lnbb[:, :], in_=bass.AP(lnb_h, 0, [[0, 128], [1, D]])))
    nc.sync.wait_ge(s_aos[0].h, s_aos[0].n)
    nc.sync.wait_ge(s_aos[1].h, s_aos[1].n)
    nc.sync.wait_ge(s_xnw[0].h, s_xnw[0].n)
    nc.sync.wait_ge(s_xnw[1].h, s_xnw[1].n)
    for t in range(NT):
        ts = t % 2
        W(nc.gpsimd, ("yg_used", ts))
        for j in range(2):
            g_ygl[ts].dma(nc.gpsimd.indirect_dma_start(
                out=yg[ts][:, j * D:(j + 1) * D], out_offset=None,
                in_=ybuf_d[:, :],
                in_offset=bass.IndirectOffsetOnAxis(
                    ap=slots[:, t * 2 + j: t * 2 + j + 1], axis=0)))
        st[("yg", t)] = (g_ygl[ts], g_ygl[ts].n)
        W(nc.sync, ("xno_used", ts))
        s_x5a[ts].dma(nc.sync.dma_start(out=xno[ts][:, :],
                                        in_=xnsp_d[t * 128:(t + 1) * 128, :]))
        st[("xno", t)] = (s_x5a[ts], s_x5a[ts].n)
        W(nc.sync, ("xrl_used", ts))
        s_xrl[ts].dma(nc.sync.dma_start(out=xrl[ts][:, :],
                                        in_=xo_h[t * 128:(t + 1) * 128, :]))
        st[("xrl", t)] = (s_xrl[ts], s_xrl[ts].n)
        W(nc.sync, ("aorl_used", ts))
        s_x5c[ts].dma(nc.sync.dma_start(out=aorl[ts][:, :],
                                        in_=aosp_d[t * 128:(t + 1) * 128, :]))
        st[("aorl", t)] = (s_x5c[ts], s_x5c[ts].n)

        W(nc.vector, ("yg", t))
        W(nc.vector, ("xno", t))
        W(nc.vector, ("xrl", t))
        W(nc.vector, ("aorl", t))
        vc.inc(nc.vector.tensor_scalar_mul(rest_[ts][:, :], yg[ts][:, 0:D],
                                           gates[:, t * 2: t * 2 + 1]), 1)
        nc.vector.wait_ge(vc.h, vc.n)
        vc.inc(nc.vector.scalar_tensor_tensor(
            rest_[ts][:, :], yg[ts][:, D:2 * D], gates[:, t * 2 + 1: t * 2 + 2],
            rest_[ts][:, :], op0=OP.mult, op1=OP.add), 1)
        nc.vector.wait_ge(vc.h, vc.n)
        st[("yg_used", ts)] = (vc, vc.n)
        vc.inc(nc.vector.tensor_add(rest_[ts][:, :], rest_[ts][:, :], xno[ts][:, :]), 1)
        st[("xno_used", ts)] = (vc, vc.n)
        nc.vector.wait_ge(vc.h, vc.n)
        nst = (D + 511) // 512
        for c_ in range(nst):
            w_ = min(512, D - c_ * 512)
            vc.inc(nc.vector.bn_stats(mvst[:, 6 * c_:6 * c_ + 6],
                                      rest_[ts][:, c_ * 512: c_ * 512 + w_]), 1)
            nc.vector.wait_ge(vc.h, vc.n)
        vc.inc(nc.vector.bn_aggr(mvst[:, 12:14], mvst[:, 0:6 * nst]), 1)
        nc.vector.wait_ge(vc.h, vc.n)
        vc.inc(nc.vector.tensor_scalar_add(mvst[:, 14:15], mvst[:, 13:14], 1e-5), 1)
        nc.scalar.wait_ge(vc.h, vc.n)
        acx.inc(nc.scalar.activation(mvst[:, 15:16], mvst[:, 14:15], AF.Sqrt), 1)
        nc.vector.wait_ge(acx.h, acx.n)
        vc.inc(nc.vector.reciprocal(mvst[:, 15:16], mvst[:, 15:16]), 1)
        nc.vector.wait_ge(vc.h, vc.n)
        os_ = t % 2
        W(nc.vector, ("outt_used", os_))
        vc.inc(nc.vector.tensor_scalar(outt[os_][:, :], rest_[ts][:, :], mvst[:, 12:13],
                                       mvst[:, 15:16], op0=OP.subtract, op1=OP.mult), 1)
        nc.vector.wait_ge(vc.h, vc.n)
        nc.vector.wait_ge(g_cst.h, g_cst.n)
        vc.inc(nc.vector.tensor_mul(outt[os_][:, :], outt[os_][:, :], lngb[:, :]), 1)
        nc.vector.wait_ge(vc.h, vc.n)
        vc.inc(nc.vector.tensor_add(outt[os_][:, :], outt[os_][:, :], lnbb[:, :]), 1)
        nc.vector.wait_ge(vc.h, vc.n)
        vc.inc(nc.vector.tensor_add(outt[os_][:, :], outt[os_][:, :],
                                    xrl[ts][:, :]), 1)
        st[("xrl_used", ts)] = (vc, vc.n)
        nc.vector.wait_ge(vc.h, vc.n)
        vc.inc(nc.vector.tensor_add(outt[os_][:, :], outt[os_][:, :],
                                    aorl[ts][:, :]), 1)
        st[("aorl_used", ts)] = (vc, vc.n)
        st[("outv", t)] = (vc, vc.n)
        W(nc.sync, ("outv", t))
        s_ost[os_].dma(nc.sync.dma_start(out=out_h[t * 128:(t + 1) * 128, :],
                                         in_=outt[os_][:, :]))
        st[("outt_used", os_)] = (s_ost[os_], s_ost[os_].n)

    nc.sync.wait_ge(s_ost[0].h, s_ost[0].n)
    nc.sync.wait_ge(s_ost[1].h, s_ost[1].n)
    stk.close()
    return nc


def _host_prep(cfg, x_own, x_oth, W_qkv, W_out, wg, fc1_w, fc1_b, fc2_w, fc2_b,
               ln_g, ln_b, core_id):
    D = cfg["D"]; E = cfg["E"]; PAD = cfg["PAD"]; NC_ = cfg["NCORES"]
    GIW = PAD // 16
    f32 = np.float32
    Wq = np.ascontiguousarray(W_qkv[0::3, :].T, dtype=f32)
    Wk = np.ascontiguousarray(W_qkv[1::3, :].T, dtype=f32)
    Wv = np.ascontiguousarray(W_qkv[2::3, :].T, dtype=f32)
    Wo = np.ascontiguousarray(W_out.T, dtype=f32)
    E_, H_, D_ = fc1_w.shape
    DCh, HCh = D_ // 128, H_ // 128
    HSh = min(H_, 1024); NHSh = H_ // HSh
    fc1s = np.ascontiguousarray(
        fc1_w.transpose(0, 2, 1).reshape(E_, DCh, 128, NHSh, HSh)
        .transpose(0, 3, 2, 1, 4).reshape(E_ * NHSh * 128, DCh * HSh),
        dtype=f32)
    fc2r = np.ascontiguousarray(
        fc2_w.reshape(E_, HCh, 128, D_).transpose(0, 2, 1, 3)
        .reshape(E_ * 128, HCh * D_), dtype=f32)
    tri = (np.arange(128)[None, :] > np.arange(128)[:, None]).astype(f32)
    ident = np.eye(128, dtype=f32)
    e8 = np.broadcast_to(np.arange(E, dtype=f32)[None, :], (128, E)).copy()
    epad = np.broadcast_to(
        np.array([e * PAD for j in range(2) for e in range(E)], f32)[None, :],
        (128, 2 * E)).copy()
    pmask = np.zeros((1, 16 * NC_), f32)
    pmask[0, :16 * core_id] = 1.0
    gi = (np.arange(E * GIW, dtype=np.int64)[None, :] * 16
          + (np.arange(128) % 16)[:, None]).astype(np.int16)
    return {
        "x_own": np.ascontiguousarray(x_own, dtype=f32),
        "x_oth": np.ascontiguousarray(x_oth, dtype=f32),
        "wq_t": Wq, "wk_t": Wk, "wv_t": Wv, "wo_t": Wo,
        "wg": np.ascontiguousarray(wg, dtype=f32),
        "fc1s": fc1s, "fc2r": fc2r,
        "fc1b": np.ascontiguousarray(fc1_b, dtype=f32),
        "fc2b": np.ascontiguousarray(fc2_b, dtype=f32),
        "ln_g": np.ascontiguousarray(ln_g.reshape(1, D), dtype=f32),
        "ln_b": np.ascontiguousarray(ln_b.reshape(1, D), dtype=f32),
        "tri": tri, "ident": ident, "e8": e8, "epad_bc": epad,
        "pmask": pmask, "gidx": gi,
    }


def finalize_for_run(nc):
    mask = {}
    for lib in all_libraries:
        for it in lib.instructions:
            mask[it] = mask.get(it, 0) | (1 << lib.index)
    _bass_rust.insert_library_loads(nc, mask, len(all_libraries), _std_lib.index)
    mybir.codegen_inst_isa_subclasses(nc)


def kernel(L0_attn_x, L0_attn_W_qkv, L0_attn_W_out, L0_ffn_wg, L0_ffn_fc1_w,
           L0_ffn_fc1_b, L0_ffn_fc2_w, L0_ffn_fc2_b, L0_ffn_ln_gamma,
           L0_ffn_ln_beta):
    cfg = dict(FULL_CFG); cfg['p2_early'] = False
    B, S, D = 4, 4096, 1024
    TOK = cfg["TOK"]
    x = np.asarray(L0_attn_x, np.float32)
    nc = build_program(cfg)
    finalize_for_run(nc)
    in_maps = []
    for c in range(8):
        b, h = c // 2, c % 2
        x_own = x[b, h * TOK:(h + 1) * TOK, :]
        x_oth = x[b, (1 - h) * TOK:(2 - h) * TOK, :]
        in_maps.append(_host_prep(
            cfg, x_own, x_oth, np.asarray(L0_attn_W_qkv), np.asarray(L0_attn_W_out),
            np.asarray(L0_ffn_wg), np.asarray(L0_ffn_fc1_w), np.asarray(L0_ffn_fc1_b),
            np.asarray(L0_ffn_fc2_w), np.asarray(L0_ffn_fc2_b),
            np.asarray(L0_ffn_ln_gamma), np.asarray(L0_ffn_ln_beta), c))
    res = run_bass_kernel_spmd(nc, in_maps, core_ids=list(range(8)))
    outs = [np.asarray(r["out"]) for r in res.results]
    return np.concatenate(outs, 0).reshape(B, S, D).astype(np.float32)
